# version 05
# baseline (speedup 1.0000x reference)
import numpy as np
import contextlib
import concourse.bass as bass
import concourse.mybir as mybir

F32 = mybir.dt.float32
BF16 = mybir.dt.bfloat16
AF = mybir.ActivationFunctionType
ALU = mybir.AluOpType
AX = mybir.AxisListType

FULL = dict(T=8192, TQ=2048, DE=2816, TF=512)
MINI = dict(T=1024, TQ=256, DE=256, TF=256)

D = 1024
NH = 16
NE = 8
TC = 256
EXP_SCALE = 0.125
EXP_BIAS = -8.0
LN_EPS = 1e-5
RMS_EPS = 1e-6


def derived(cfg):
    c = dict(cfg)
    c["NTC"] = c["TQ"] // TC
    c["NSCH"] = c["T"] // 256
    c["SRES"] = c["T"] // 2
    c["NRES"] = c["SRES"] // 256
    c["NVG"] = (c["T"] - c["SRES"]) // 256
    c["NNS"] = c["DE"] // 128
    c["NTF"] = c["TQ"] // c["TF"]
    c["NTSF"] = c["TF"] // 128
    c["NTS"] = c["TQ"] // 128
    c["NGH"] = c["NTC"] * NH
    c["NST"] = c["T"] // 128
    return c


class Chain:

    def __init__(self, sems):
        self.sems = sems
        self.steps = []
        self.marks = {}

    def step(self, engine, fn, extra=()):
        self.steps.append((engine, fn, list(extra)))

    def mark(self, key):
        self.marks[key] = len(self.steps)

    def _count(self, idx, engine):
        return sum(1 for e, _, _ in self.steps[:idx] if e == engine)

    def milestone(self, key):
        idx = self.marks[key]
        return [(self.sems[e], self._count(idx, e)) for e in self.sems]

    def emit_range(self, engine, eng, lo_mark, hi_mark):
        lo = self.marks[lo_mark]
        hi = self.marks[hi_mark]
        for i in range(lo, hi):
            e, fn, extra = self.steps[i]
            if e != engine:
                continue
            for semx, val in extra:
                if val > 0:
                    eng.wait_ge(semx, val)
            for ename, semx in self.sems.items():
                vcnt = self._count(i, ename)
                if vcnt > 0:
                    eng.wait_ge(semx, vcnt)
            fn(eng).then_inc(self.sems[e], 1)


def build_program(cfg, phase_mode="all", no_vdma=False, no_ktw=False):
    c = derived(cfg)
    T, TQ, DE, TF = c["T"], c["TQ"], c["DE"], c["TF"]
    NTC, NSCH, SRES, NRES, NVG = c["NTC"], c["NSCH"], c["SRES"], c["NRES"], c["NVG"]
    NNS, NTF, NTSF, NTS, NGH, NST = c["NNS"], c["NTF"], c["NTSF"], c["NTS"], c["NGH"], c["NST"]
    NRT = SRES // 128

    nc = bass.Bass("TRN2", target_bir_lowering=False)

    xb = nc.dram_tensor("xb", [T, D], F32, kind="ExternalInput").ap()
    wat = nc.dram_tensor("wat", [NH * D, D], F32, kind="ExternalInput").ap()
    w1t = nc.dram_tensor("w1t", [NE, NNS, D, 128], F32, kind="ExternalInput").ap()
    w2t = nc.dram_tensor("w2t", [NE, NNS, D, 128], F32, kind="ExternalInput").ap()
    w3t = nc.dram_tensor("w3t", [NE, DE, D], F32, kind="ExternalInput").ap()
    gwt = nc.dram_tensor("gwt", [D, NE], F32, kind="ExternalInput").ap()
    rmswb = nc.dram_tensor("rmswb", [128, D], F32, kind="ExternalInput").ap()
    eye = nc.dram_tensor("eye", [128, 128], F32, kind="ExternalInput").ap()
    out = nc.dram_tensor("out", [TQ, D], F32, kind="ExternalOutput").ap()

    h1s = nc.dram_tensor("h1s", [T, D], BF16, kind="Internal").ap()
    h1d = nc.dram_tensor("h1d", [8, T, 128], BF16, kind="Internal").ap()
    h1t = nc.dram_tensor("h1t", [D, T], BF16, kind="Internal").ap()
    watb = nc.dram_tensor("watb", [NH * D, D], BF16, kind="Internal").ap()
    w1b = nc.dram_tensor("w1b", [NE, NNS, D, 128], BF16, kind="Internal").ap()
    w2b = nc.dram_tensor("w2b", [NE, NNS, D, 128], BF16, kind="Internal").ap()
    w3b = nc.dram_tensor("w3b", [NE, DE, D], BF16, kind="Internal").ap()
    gwb = nc.dram_tensor("gwb", [D, NE], BF16, kind="Internal").ap()
    x2s = nc.dram_tensor("x2s", [TQ, D], F32, kind="Internal").ap()
    yd = nc.dram_tensor("yd", [8, TQ, 128], BF16, kind="Internal").ap()

    es = contextlib.ExitStack()
    with es:
        def sem(name):
            return es.enter_context(nc.semaphore(name))

        s_x = [sem("s_x0"), sem("s_x1")]
        s_h1w = sem("s_h1w"); s_hw = [sem("s_hw0"), sem("s_hw1")]
        s_tr = sem("s_tr"); s_h1t = sem("s_h1t")
        s_kt = [sem("s_kt0"), sem("s_kt1")]
        s_w = sem("s_w"); s_v = [sem(f"s_v{i}") for i in range(4)]
        s_xr = [sem("s_xr0"), sem("s_xr1")]
        s_x2w = sem("s_x2w"); s_yw = sem("s_yw"); s_cw = sem("s_cw"); s_cf = sem("s_cf")
        s_yt = sem("s_yt")
        s_w1s = [sem("s_w1a"), sem("s_w1b")]; s_w2s = [sem("s_w2a"), sem("s_w2b")]
        s_f3s = [sem(f"s_f3{i}") for i in range(4)]
        s_x4 = [sem("s_x4a"), sem("s_x4b")]
        s_out = sem("s_out"); s_rw = sem("s_rw"); s_eye = sem("s_eye")
        s_smm = sem("s_smm"); s_exp = sem("s_exp"); s_vd = sem("s_vd")
        s_potev = sem("s_potev"); s_pjmm = sem("s_pjmm"); s_pjev = sem("s_pjev")
        s_dns = sem("s_dns"); s_trd = sem("s_trd"); s_rt = sem("s_rt")
        s_p1s = sem("s_p1s"); s_p1v = sem("s_p1v")
        s_ts = sem("s_ts"); s_tv = sem("s_tv")
        s_gs = sem("s_gs"); s_gv = sem("s_gv")
        s_4s = sem("s_4s"); s_4v = sem("s_4v")
        s_vv = sem("s_vv"); s_one = sem("s_one")
        s_lgm = sem("s_lgm"); s_lge = sem("s_lge")
        s_fupm = sem("s_fupm"); s_fupg = sem("s_fupg"); s_fsil = sem("s_fsil")
        s_fhm = sem("s_fhm"); s_fsc = sem("s_fsc"); s_fcm = sem("s_fcm")
        s_f3c = sem("s_f3c")

        ONES = es.enter_context(nc.sbuf_tensor("ONES", [128, 128], BF16))
        EYE = es.enter_context(nc.sbuf_tensor("EYE", [128, 128], F32))
        EBIA = es.enter_context(nc.sbuf_tensor("EBIA", [128, 1], F32))
        DUMP = es.enter_context(nc.sbuf_tensor("DUMP", [128, D], BF16))
        RMSWB = es.enter_context(nc.sbuf_tensor("RMSWB", [128, D], F32))

        s_p1s2 = sem("s_p1s2"); s_p1v2 = sem("s_p1v2")
        p1 = Chain({"scalar": s_p1s, "vector": s_p1v})
        p1b = Chain({"scalar": s_p1s2, "vector": s_p1v2})
        tl = Chain({"scalar": s_ts, "vector": s_tv})
        gt = Chain({"scalar": s_gs, "vector": s_gv})
        p4 = Chain({"scalar": s_4s, "vector": s_4v})

        a_es = contextlib.ExitStack()
        with a_es:
            def sb(name, shape, dt):
                return a_es.enter_context(nc.sbuf_tensor(name, shape, dt))
            H1R = sb("H1R", [128, NRT * D], BF16)
            XST = [sb(f"XST{i}", [128, D], F32) for i in range(2)]
            H1ST = [sb(f"H1ST{i}", [128, D], BF16) for i in range(2)]
            TR = sb("TR", [128, T], BF16)
            KTB = [sb(f"KTB{i}", [64, T], BF16) for i in range(2)]
            WBUF = sb("WBUF", [128, 8 * D], BF16)
            VR = sb("VR", [128, 8 * D], BF16)
            EB = sb("EB", [128, 1024], BF16)
            OTS = sb("OTS", [128, 8 * TC], BF16)
            DNS = sb("DNS", [128, TC], F32)
            RECT = [sb(f"RECT{i}", [128, 2], F32) for i in range(2)]
            PJA = [sb(f"PJA{i}", [128, 2 * D], F32) for i in range(2)]
            XO = [sb(f"XO{i}", [128, D], F32) for i in range(2)]
            X2T = sb("X2T", [128, D], F32)
            H2T = sb("H2T", [128, D], F32)
            YTL = sb("YTL", [128, D], BF16)
            stnames = ("S1", "S2", "MEAN", "VEPS", "R0", "Y0", "T1", "T3", "RSTD", "BIAS")
            STT = {k: sb("st" + k, [128, 1], F32) for k in stnames}
            STB = {k: sb("sb" + k, [128, 1], F32) for k in stnames}
            PS = a_es.enter_context(nc.psum_tensor("PS", [128, 1024], F32))
            POT = a_es.enter_context(nc.psum_tensor("POT", [128, 8 * TC], F32))
            PDN = a_es.enter_context(nc.psum_tensor("PDN", [128, TC], F32))
            PPJ = a_es.enter_context(nc.psum_tensor("PPJ", [128, 512], F32))

            def h1dst(st):
                return H1R[:, st * D:(st + 1) * D] if st < NRT else H1ST[st % 2][:]

            def v_tile(gh, k, st2):
                stile = 2 * k + st2
                if no_vdma:
                    stile = stile % NRT
                if stile < NRT:
                    return H1R[:, stile * D:(stile + 1) * D]
                off = stile - NRT
                G = gh * NVG + off // 2
                slot = G % 4
                return VR[:, (slot * 2 + off % 2) * D:(slot * 2 + off % 2 + 1) * D]

            def ln_stats(ch, ST, src_tile, eps, with_mean, extra_first=()):
                S1, S2, MEAN, VEPS, R0, Y0, T1, T3, RSTD, BIAS = (ST[k] for k in stnames)
                ch.step("scalar", lambda s, src=src_tile: s.activation(
                    DUMP[:], src, AF.Square, accum_out=S2[:]), extra_first)
                if with_mean:
                    ch.step("vector", lambda v, src=src_tile: v.reduce_sum(S1[:], src, axis=AX.X))
                    ch.step("vector", lambda v: v.tensor_scalar(MEAN[:], S1[:], 1.0 / D, None, ALU.mult))
                    ch.step("vector", lambda v: v.tensor_mul(T1[:], MEAN[:], MEAN[:]))
                    ch.step("vector", lambda v: v.tensor_scalar(VEPS[:], S2[:], 1.0 / D, None, ALU.mult))
                    ch.step("vector", lambda v: v.tensor_sub(VEPS[:], VEPS[:], T1[:]))
                    ch.step("vector", lambda v, e=eps: v.tensor_scalar(VEPS[:], VEPS[:], e, None, ALU.add))
                else:
                    ch.step("vector", lambda v, e=eps: v.tensor_scalar(
                        VEPS[:], S2[:], 1.0 / D, e, ALU.mult, ALU.add))
                ch.step("vector", lambda v: v.reciprocal(R0[:], VEPS[:]))
                ch.step("scalar", lambda s: s.activation(Y0[:], R0[:], AF.Sqrt))
                ch.step("vector", lambda v: v.tensor_mul(T1[:], Y0[:], Y0[:]))
                ch.step("vector", lambda v: v.tensor_mul(T1[:], T1[:], VEPS[:]))
                ch.step("vector", lambda v: v.tensor_scalar(T3[:], T1[:], -0.5, 1.5, ALU.mult, ALU.add))
                ch.step("vector", lambda v: v.tensor_mul(RSTD[:], Y0[:], T3[:]))
                if with_mean:
                    ch.step("vector", lambda v: v.scalar_tensor_tensor(
                        BIAS[:], MEAN[:], -1.0, RSTD[:], ALU.mult, ALU.mult))

            def p1_chain(st):
                return p1 if st % 2 == 0 else p1b

            p1.mark("start"); p1b.mark("start")
            for st in range(NST):
                ch = p1_chain(st)
                ST = STT if st % 2 == 0 else STB
                ch.mark(f"t{st}")
                ln_stats(ch, ST, XST[st % 2][:], LN_EPS, True,
                         [(s_x[st % 2], 16 * (st // 2 + 1))])
                h1_extra = ([(s_hw[st % 2], 32 * ((st - NRT) // 2))]
                            if st >= NRT + 2 else [])
                ch.step("scalar", lambda s, st=st, ST=ST: s.activation(
                    h1dst(st), XST[st % 2][:], AF.Identity,
                    bias=ST["BIAS"][:], scale=ST["RSTD"][:]), h1_extra)
                ch.mark(f"h1_{st}")
            p1.mark("end"); p1b.mark("end")

            def p1_milestone(st):
                return p1_chain(st).milestone(f"h1_{st}")

            for tc in range(NTC):
                for tsub in range(2):
                    ts = tc * 2 + tsub
                    tl.mark(f"ts{ts}")
                    PA = PJA[tc % 2][:, tsub * D:(tsub + 1) * D]
                    ln_stats(tl, STT, PA, LN_EPS, True, [(s_pjev, 4 * NH * (tc + 1))])
                    x2_extra = [(s_xr[ts % 2], 16 * (ts // 2 + 1))]
                    if ts >= 1:
                        x2_extra.append((s_x2w, 16 * ts))
                    tl.step("vector", lambda v, ts=ts, PA=PA: v.scalar_tensor_tensor(
                        X2T[:], PA, STT["RSTD"][:], XO[ts % 2][:], ALU.mult, ALU.add), x2_extra)
                    tl.mark(f"pa_done{ts}")
                    tl.step("vector", lambda v: v.tensor_scalar(
                        X2T[:], X2T[:], STT["BIAS"][:], None, ALU.add))
                    tl.mark(f"x2_{ts}")
                    ln_stats(tl, STT, X2T[:], LN_EPS, True)
                    tl.step("scalar", lambda s: s.activation(
                        H2T[:], X2T[:], AF.Identity, bias=STT["BIAS"][:], scale=STT["RSTD"][:]))
                    tl.mark(f"x2free{ts}")
                    ln_stats(tl, STT, H2T[:], RMS_EPS, False)
                    y_extra = [(s_rw, 16)]
                    if ts >= 1:
                        y_extra.append((s_yw, 16 * ts))
                    tl.step("vector", lambda v: v.scalar_tensor_tensor(
                        YTL[:], H2T[:], STT["RSTD"][:], RMSWB[:], ALU.mult, ALU.mult), y_extra)
                    tl.mark(f"y_{ts}")
            tl.mark("end")

            blk = a_es.enter_context(nc.Block())

            @blk.sync
            def _(sy):
                sy.dma_start(out=RMSWB[:], in_=rmswb).then_inc(s_rw, 16)
                sy.dma_start(out=EYE[:], in_=eye).then_inc(s_eye, 16)
                for st in range(NST):
                    if st >= 2:
                        for smx, val in p1_milestone(st - 2):
                            sy.wait_ge(smx, val)
                    sy.dma_start(out=XST[st % 2][:],
                                 in_=xb[st * 128:(st + 1) * 128, :]).then_inc(s_x[st % 2], 16)
                sy.wait_ge(s_h1w, 32 * NRT)
                nhw0 = sum(1 for st in range(NRT, NST) if st % 2 == 0)
                nhw1 = (NST - NRT) - nhw0
                if nhw0:
                    sy.wait_ge(s_hw[0], 32 * nhw0)
                if nhw1:
                    sy.wait_ge(s_hw[1], 32 * nhw1)
                for dt in range(8):
                    if dt >= 1:
                        sy.wait_ge(s_h1t, 16 * dt)
                    sy.dma_start_transpose(out=TR[:], in_=h1d[dt]).then_inc(s_tr, 16)
                    sy.wait_ge(s_tr, 16 * (dt + 1))
                    sy.dma_start(out=h1t[dt * 128:(dt + 1) * 128, :], in_=TR[:]).then_inc(s_h1t, 16)
                def xo_loads(sy, tcx):
                    for tsub in range(2):
                        ts = tcx * 2 + tsub
                        if ts >= 2:
                            for smx, val in tl.milestone(f"x2_{ts - 2}"):
                                sy.wait_ge(smx, val)
                        sy.dma_start(out=XO[ts % 2][:],
                                     in_=xb[ts * 128:(ts + 1) * 128, :]).then_inc(s_xr[ts % 2], 16)

                def kt_load(sy, gh):
                    if gh >= NGH:
                        return
                    hh_ = gh % NH
                    if gh >= 2:
                        sy.wait_ge(s_smm, (gh - 1) * NSCH)
                    sy.wait_ge(s_h1t, 16 * (hh_ // 2 + 1))
                    sy.dma_start(out=KTB[gh % 2][0:64, :],
                                 in_=h1t[hh_ * 64:(hh_ + 1) * 64, :]).then_inc(s_kt[gh % 2], 16)

                kt_load(sy, 0)
                for tc in range(NTC):
                    if tc >= 1:
                        xo_loads(sy, tc - 1)
                    for h in range(NH):
                        gh = tc * NH + h
                        kt_load(sy, gh + 1)
                        for g in range(0 if no_vdma else NVG):
                            G = gh * NVG + g
                            if G >= 4:
                                ghp, gp = divmod(G - 4, NVG)
                                sy.wait_ge(s_vd, ghp * NSCH + NRES + gp + 1)
                            r0 = SRES + g * 256
                            sy.dma_start(
                                out=VR[:, ((G % 4) * 2) * D:((G % 4) * 2 + 2) * D]
                                .rearrange("p (t d) -> p t d", d=D),
                                in_=h1s[r0:r0 + 256, :].rearrange("(t p) d -> p t d", p=128)
                            ).then_inc(s_v[G % 4], 16)
                        if gh >= 1:
                            sy.wait_ge(s_pjmm, 4 * gh)
                        if tc == 0:
                            sy.wait_ge(s_cw, 16 * (h + 1))
                        sy.dma_start(out=WBUF[:].rearrange("p (c d) -> p c d", d=D),
                                     in_=watb[h * D:(h + 1) * D, :]
                                     .rearrange("(c p) d -> p c d", p=128)).then_inc(s_w, 16)
                xo_loads(sy, NTC - 1)

            @blk.gpsimd
            def _(g):
                for st in range(NST):
                    for smx, val in p1_milestone(st):
                        g.wait_ge(smx, val)
                    src = h1dst(st)
                    dsem = s_h1w if st < NRT else s_hw[st % 2]
                    g.dma_start(out=h1s[st * 128:(st + 1) * 128, :], in_=src).then_inc(dsem, 16)
                    g.dma_start(out=h1d[:, st * 128:(st + 1) * 128, :].rearrange("dt p d -> p dt d"),
                                in_=src.rearrange("p (dt d) -> p dt d", d=128)).then_inc(dsem, 16)
                for ci in range(NH):
                    if ci >= 1:
                        g.wait_ge(s_cw, 16 * ci)
                    g.dma_start(out=watb[ci * D:(ci + 1) * D, :],
                                in_=wat[ci * D:(ci + 1) * D, :]).then_inc(s_cw, 16)
                g.dma_start(out=gwb, in_=gwt).then_inc(s_cf, 16)
                for e in range(NE):
                    g.dma_start(out=w1b[e], in_=w1t[e]).then_inc(s_cf, 16)
                    g.dma_start(out=w2b[e], in_=w2t[e]).then_inc(s_cf, 16)
                    g.dma_start(out=w3b[e], in_=w3t[e]).then_inc(s_cf, 16)
                for ts in range(2 * NTC):
                    for smx, val in tl.milestone(f"x2free{ts}"):
                        g.wait_ge(smx, val)
                    g.dma_start(out=x2s[ts * 128:(ts + 1) * 128, :], in_=X2T[:]).then_inc(s_x2w, 16)
                    for smx, val in tl.milestone(f"y_{ts}"):
                        g.wait_ge(smx, val)
                    g.dma_start(out=yd[:, ts * 128:(ts + 1) * 128, :].rearrange("dt p d -> p dt d"),
                                in_=YTL[:].rearrange("p (dt d) -> p dt d", d=128)).then_inc(s_yw, 16)

            @blk.tensor
            def _(t):
                t.wait_ge(s_one, 2)
                t.wait_ge(s_eye, 16)

                def emit_S(gh, tc, k):
                    sg = gh * NSCH + k
                    if k == 0:
                        t.wait_ge(s_kt[gh % 2], 16 * (gh // 2 + 1))
                    if sg >= 2:
                        t.wait_ge(s_exp, sg - 1)
                    KT = KTB[gh % 2]
                    base = (k % 2) * 512
                    t.matmul(PS[:, base:base + TC],
                             KT[0:64, (2 * k) * 128:(2 * k + 1) * 128],
                             KT[0:64, tc * TC:(tc + 1) * TC], start=True, stop=True)
                    t.matmul(PS[:, base + TC:base + 512],
                             KT[0:64, (2 * k + 1) * 128:(2 * k + 2) * 128],
                             KT[0:64, tc * TC:(tc + 1) * TC], start=True, stop=True).then_inc(s_smm, 1)

                def emit_V(gh, k):
                    sg = gh * NSCH + k
                    t.wait_ge(s_exp, sg + 1)
                    if k == 0 and gh >= 1:
                        t.wait_ge(s_potev, 2 * gh)
                        t.wait_ge(s_dns, gh)
                    if k >= NRES and not no_vdma:
                        G = gh * NVG + (k - NRES)
                        t.wait_ge(s_v[G % 4], 16 * (G // 4 + 1))
                    last_k = (k == NSCH - 1)
                    ins = None
                    for st2 in range(2):
                        rhs = EB[:, ((k % 2) * 2 + st2) * TC:((k % 2) * 2 + st2 + 1) * TC]
                        first = (k == 0 and st2 == 0)
                        last = (last_k and st2 == 1)
                        t.matmul(PDN[:], ONES[:], rhs, start=first, stop=last)
                        vt = v_tile(gh, k, st2)
                        for cc in range(8):
                            ins = t.matmul(POT[:, cc * TC:(cc + 1) * TC],
                                           vt[:, cc * 128:(cc + 1) * 128], rhs,
                                           start=(first and cc % 2 == 0),
                                           stop=(last and cc % 2 == 1))
                    ins.then_inc(s_vd, 1)

                def emit_pq(pgh, q):
                    tsub, dh = divmod(q, 2)
                    t.wait_ge(s_potev, 2 * (pgh + 1))
                    t.wait_ge(s_w, 16 * (pgh + 1))
                    t.wait_ge(s_pjev, pgh * 4 + q)
                    if q == 0:
                        t.wait_ge(s_rt, pgh + 1)
                    ins = None
                    for cc in range(8):
                        ins = t.matmul(PPJ[:],
                                       OTS[:, cc * TC + tsub * 128:cc * TC + (tsub + 1) * 128],
                                       WBUF[:, cc * D + dh * 512:cc * D + (dh + 1) * 512],
                                       start=(cc == 0), stop=(cc == 7))
                    ins.then_inc(s_pjmm, 1)

                pos2 = {}
                for q in range(4):
                    kq = max(1, min(NSCH - 1, (q + 1) * NSCH // 5))
                    pos2.setdefault(kq, []).append(q)

                def emit_trd(pgh):
                    t.wait_ge(s_dns, pgh + 1)
                    t.wait_ge(s_pjev, 4 * pgh)
                    t.transpose(PPJ[:, 0:128], DNS[:, 0:128], EYE[:])
                    t.transpose(PPJ[:, 128:256], DNS[:, 128:256], EYE[:]).then_inc(s_trd, 1)

                for tc in range(NTC):
                    for h in range(NH):
                        gh = tc * NH + h
                        emit_S(gh, tc, 0)
                        for k in range(1, NSCH):
                            emit_S(gh, tc, k)
                            if k == 1 and gh >= 1:
                                emit_trd(gh - 1)
                            if gh >= 1:
                                for q in pos2.get(k, ()):
                                    emit_pq(gh - 1, q)
                            emit_V(gh, k - 1)
                        emit_V(gh, NSCH - 1)
                emit_trd(NGH - 1)
                for q in range(4):
                    emit_pq(NGH - 1, q)

            @blk.scalar
            def _(s):
                s.wait_ge(s_one, 2)
                for st in range(NST):
                    p1_chain(st).emit_range("scalar", s, f"t{st}", f"h1_{st}")
                for tc in range(NTC):
                    for h in range(NH):
                        gh = tc * NH + h
                        for k in range(NSCH):
                            sg = gh * NSCH + k
                            s.wait_ge(s_smm, sg + 1)
                            if sg >= 2:
                                s.wait_ge(s_vd, sg - 1)
                            s.activation(EB[:, (k % 2) * 512:(k % 2) * 512 + 512],
                                         PS[:, (k % 2) * 512:(k % 2) * 512 + 512],
                                         AF.Exp, scale=EXP_SCALE, bias=EBIA[:]).then_inc(s_exp, 1)
                        s.wait_ge(s_vd, (gh + 1) * NSCH)
                        if gh >= 1:
                            s.wait_ge(s_pjmm, 4 * gh)
                        s.activation(OTS[:, 0:4 * TC], POT[:, 0:4 * TC],
                                     AF.Identity).then_inc(s_potev, 1)
                        if h == 1 and tc >= 1:
                            tl.emit_range("scalar", s, f"ts{(tc - 1) * 2}", f"y_{(tc - 1) * 2 + 1}")
                tl.emit_range("scalar", s, f"ts{(NTC - 1) * 2}", "end")

            @blk.vector
            def _(v):
                vv = [0]

                def vsync(ins):
                    vv[0] += 1
                    ins.then_inc(s_vv, 1)
                    v.wait_ge(s_vv, vv[0])

                v.memset(ONES[:], 1.0).then_inc(s_one, 1)
                v.memset(EBIA[:], EXP_BIAS).then_inc(s_one, 1)
                for st in range(NST):
                    p1_chain(st).emit_range("vector", v, f"t{st}", f"h1_{st}")

                def emit_dns(gh):
                    v.wait_ge(s_vd, (gh + 1) * NSCH)
                    v.tensor_copy(DNS[:], PDN[:]).then_inc(s_dns, 1)
                    if gh >= 1:
                        v.wait_ge(s_pjmm, 4 * gh)
                    v.tensor_copy(OTS[:, 4 * TC:8 * TC],
                                  POT[:, 4 * TC:8 * TC]).then_inc(s_potev, 1)

                def emit_recip2(pgh):
                    v.wait_ge(s_trd, pgh + 1)
                    vsync(v.reciprocal(RECT[pgh % 2][:, 0:1], PPJ[:, 0:1]))
                    ins = v.reciprocal(RECT[pgh % 2][:, 1:2], PPJ[:, 128:129])
                    ins.then_inc(s_rt, 1)
                    vv[0] += 0

                def emit_adds(pgh):
                    ptc = pgh // NH
                    for q in range(4):
                        tsub, dh = divmod(q, 2)
                        dst = PJA[ptc % 2][:, tsub * D + dh * 512:tsub * D + (dh + 1) * 512]
                        v.wait_ge(s_pjmm, pgh * 4 + q + 1)
                        rsc = RECT[pgh % 2][:, tsub:tsub + 1]
                        if pgh % NH == 0:
                            if ptc >= 2:
                                for smx, val in tl.milestone(f"pa_done{(ptc - 2) * 2 + 1}"):
                                    v.wait_ge(smx, val)
                            ins = v.tensor_scalar(dst, PPJ[:], rsc, None, ALU.mult)
                        else:
                            ins = v.scalar_tensor_tensor(dst, PPJ[:], rsc, dst, ALU.mult, ALU.add)
                        ins.then_inc(s_pjev, 1)

                for tc in range(NTC):
                    for h in range(NH):
                        gh = tc * NH + h
                        if gh >= 1:
                            emit_recip2(gh - 1)
                            emit_adds(gh - 1)
                        emit_dns(gh)
                        if h == 1 and tc >= 1:
                            tl.emit_range("vector", v, f"ts{(tc - 1) * 2}", f"y_{(tc - 1) * 2 + 1}")
                emit_recip2(NGH - 1)
                emit_adds(NGH - 1)
                tl.emit_range("vector", v, f"ts{(NTC - 1) * 2}", "end")

        if phase_mode == "attn":
            with nc.Block() as blk2s:
                @blk2s.sync
                def _(sy):
                    eng = sy
                    eng.wait_ge(s_yw, 16 * 2 * NTC)
                    eng.wait_ge(s_x2w, 16 * 2 * NTC)
                    for smx, val in tl.milestone("end"):
                        eng.wait_ge(smx, val)
                    for ts in range(NTS):
                        sy.dma_start(out=out[ts * 128:(ts + 1) * 128, :],
                                     in_=x2s[ts * 128:(ts + 1) * 128, :]).then_inc(s_out, 16)
                    sy.wait_ge(s_out, 16 * NTS)
            return nc

        f_es = contextlib.ExitStack()
        with f_es:
            def fb(name, shape, dt):
                return f_es.enter_context(nc.sbuf_tensor(name, shape, dt))
            YT = fb("YT", [128, 8 * TQ], BF16)
            HALL = fb("HALL", [128, NNS * TF], BF16)
            COMB = fb("COMB", [128, NTS * D], F32)
            W1R = [fb(f"W1R{i}", [128, 8 * 128], BF16) for i in range(2)]
            W2R = [fb(f"W2R{i}", [128, 8 * 128], BF16) for i in range(2)]
            W3R = fb("W3R", [128, 4 * D], BF16)
            SU = [fb(f"SU{i}", [128, TF], BF16) for i in range(2)]
            CT = [fb(f"CT{i}", [128, 512], F32) for i in range(2)]
            GWB = fb("GWB", [128, 8 * NE], BF16)
            LGA = fb("LGA", [128, NTS * NE], F32)
            WALL = fb("WALL", [128, NTS * NE], F32)
            X2R = [fb(f"X2R{i}", [128, D], F32) for i in range(2)]
            H2P = fb("H2P", [128, D], F32)
            FFT = fb("FFT", [128, D], F32)
            FLN = fb("FLN", [128, D], F32)
            OUTT = fb("OUTT", [128, D], F32)
            G_ = {k: fb("g" + k, [128, NE], F32) for k in ("MK1", "LM", "MK2", "EL", "TMP")}
            G1 = {k: fb("g1" + k, [128, 1], F32)
                  for k in ("M1", "M2", "NM1", "D21", "E2", "DN2", "RCW")}
            ST2 = {k: fb("s2" + k, [128, 1], F32) for k in stnames}
            PUP = [f_es.enter_context(nc.psum_tensor(f"PUP{i}", [128, TF], F32)) for i in range(2)]
            PGT = [f_es.enter_context(nc.psum_tensor(f"PGT{i}", [128, TF], F32)) for i in range(2)]
            POUT = f_es.enter_context(nc.psum_tensor("POUT", [128, NTSF * 512], F32))
            PL = POUT[:, 0:NE]

            NFC = 1 + 3 * NE

            def entry_waits(eng):
                eng.wait_ge(s_vd, NGH * NSCH)
                eng.wait_ge(s_exp, NGH * NSCH)
                eng.wait_ge(s_pjmm, 4 * NGH)
                eng.wait_ge(s_pjev, 4 * NGH)
                eng.wait_ge(s_potev, 2 * NGH)
                eng.wait_ge(s_dns, NGH)
                eng.wait_ge(s_trd, NGH)
                eng.wait_ge(s_rt, NGH)
                eng.wait_ge(s_yw, 16 * 2 * NTC)
                eng.wait_ge(s_x2w, 16 * 2 * NTC)
                eng.wait_ge(s_kt[0], 16 * ((NGH + 1) // 2))
                eng.wait_ge(s_kt[1], 16 * (NGH // 2))
                eng.wait_ge(s_w, 16 * NGH)
                ngv = 0 if no_vdma else NGH * NVG
                for si in range(4):
                    cnt = (ngv - si + 3) // 4
                    if cnt > 0:
                        eng.wait_ge(s_v[si], 16 * cnt)
                for smx, val in tl.milestone("end"):
                    eng.wait_ge(smx, val)

            for tsg in range(NTS):
                gt.mark(f"tsg{tsg}")
                LG = LGA[:, tsg * NE:(tsg + 1) * NE]
                WF = WALL[:, tsg * NE:(tsg + 1) * NE]
                gt.step("vector", lambda v, LG=LG: v.reduce_max(G1["M1"][:], LG, axis=AX.X),
                        [(s_lge, tsg + 1)])
                gt.step("vector", lambda v, LG=LG: v.tensor_scalar(
                    G_["MK1"][:], LG, G1["M1"][:], None, ALU.is_ge))
                gt.step("vector", lambda v, LG=LG: v.scalar_tensor_tensor(
                    G_["LM"][:], G_["MK1"][:], -1e30, LG, ALU.mult, ALU.add))
                gt.step("vector", lambda v: v.reduce_max(G1["M2"][:], G_["LM"][:], axis=AX.X))
                gt.step("vector", lambda v, LG=LG: v.tensor_scalar(
                    G_["MK2"][:], LG, G1["M2"][:], None, ALU.is_ge))
                gt.step("vector", lambda v: v.tensor_scalar_mul(G1["NM1"][:], G1["M1"][:], -1.0))
                gt.step("scalar", lambda s, LG=LG: s.activation(
                    G_["EL"][:], LG, AF.Exp, bias=G1["NM1"][:]))
                gt.step("vector", lambda v: v.tensor_sub(G1["D21"][:], G1["M2"][:], G1["M1"][:]))
                gt.step("scalar", lambda s: s.activation(G1["E2"][:], G1["D21"][:], AF.Exp))
                gt.step("vector", lambda v: v.tensor_scalar(
                    G1["DN2"][:], G1["E2"][:], 1.0, None, ALU.add))
                gt.step("vector", lambda v: v.reciprocal(G1["RCW"][:], G1["DN2"][:]))
                gt.step("vector", lambda v: v.tensor_scalar(
                    G_["TMP"][:], G_["EL"][:], G1["RCW"][:], None, ALU.mult))
                gt.step("vector", lambda v, WF=WF: v.tensor_mul(WF, G_["TMP"][:], G_["MK2"][:]))
            gt.mark("end")

            for ts in range(NTS):
                tf = ts // NTSF
                p4.mark(f"ts{ts}")
                ln_extra = [(s_x4[ts % 2], 16 * (ts // 2 + 1))]
                ln_stats(p4, ST2, X2R[ts % 2][:], LN_EPS, True, ln_extra)
                p4.step("scalar", lambda s, ts=ts: s.activation(
                    H2P[:], X2R[ts % 2][:], AF.Identity, bias=ST2["BIAS"][:], scale=ST2["RSTD"][:]))
                p4.step("vector", lambda v, ts=ts: v.tensor_add(
                    FFT[:], H2P[:], COMB[:, ts * D:(ts + 1) * D]),
                    [(s_fcm, (tf + 1) * 2 * NE * NTSF)])
                ln_stats(p4, ST2, FFT[:], LN_EPS, True)
                p4.step("scalar", lambda s: s.activation(
                    FLN[:], FFT[:], AF.Identity, bias=ST2["BIAS"][:], scale=ST2["RSTD"][:]))
                p4.step("vector", lambda v, ts=ts: v.tensor_add(OUTT[:], X2R[ts % 2][:], FLN[:]),
                        [(s_out, 16 * ts)] if ts >= 1 else [])
                p4.mark(f"out{ts}")
            p4.mark("end")

            blk2 = f_es.enter_context(nc.Block())

            @blk2.sync
            def _(sy):
                entry_waits(sy)
                sy.wait_ge(s_cf, 16 * NFC)
                for dt in range(8):
                    sy.dma_start_transpose(out=YT[:, dt * TQ:(dt + 1) * TQ],
                                           in_=yd[dt]).then_inc(s_yt, 16)
                sy.dma_start(out=GWB[:].rearrange("p (dt e) -> p dt e", e=NE),
                             in_=gwb.rearrange("(dt p) e -> p dt e", p=128)).then_inc(s_rw, 16)
                for tf in range(NTF):
                    for e in range(NE):
                        fe = tf * NE + e
                        for ns in range(NNS):
                            u = fe * NNS + ns
                            if u >= 2:
                                sy.wait_ge(s_fupm, u - 1)
                            sy.dma_start(out=W1R[ns % 2][:].rearrange("p (dt n) -> p dt n", n=128),
                                         in_=w1b[e, ns].rearrange("(dt p) n -> p dt n", p=128)
                                         ).then_inc(s_w1s[ns % 2], 16)
                            if u >= 2:
                                sy.wait_ge(s_fupg, u - 1)
                            sy.dma_start(out=W2R[ns % 2][:].rearrange("p (dt n) -> p dt n", n=128),
                                         in_=w2b[e, ns].rearrange("(dt p) n -> p dt n", p=128)
                                         ).then_inc(s_w2s[ns % 2], 16)
                        for mh in range(2):
                            for ns in range(NNS):
                                j = (fe * 2 + mh) * NNS + ns
                                if j >= 4:
                                    sy.wait_ge(s_f3c, j - 3)
                                sy.dma_start(out=W3R[:, (j % 4) * D:(j % 4 + 1) * D],
                                             in_=w3b[e, ns * 128:(ns + 1) * 128, :]
                                             ).then_inc(s_f3s[j % 4], 16)
                sy.wait_ge(s_out, 16 * NTS)

            @blk2.gpsimd
            def _(g):
                entry_waits(g)
                for ts in range(NTS):
                    if ts >= 2:
                        for smx, val in p4.milestone(f"out{ts - 2}"):
                            g.wait_ge(smx, val)
                    g.dma_start(out=X2R[ts % 2][:],
                                in_=x2s[ts * 128:(ts + 1) * 128, :]).then_inc(s_x4[ts % 2], 16)
                    if ts >= 1:
                        for smx, val in p4.milestone(f"out{ts - 1}"):
                            g.wait_ge(smx, val)
                        g.dma_start(out=out[(ts - 1) * 128:ts * 128, :],
                                    in_=OUTT[:]).then_inc(s_out, 16)
                for smx, val in p4.milestone(f"out{NTS - 1}"):
                    g.wait_ge(smx, val)
                g.dma_start(out=out[(NTS - 1) * 128:NTS * 128, :],
                            in_=OUTT[:]).then_inc(s_out, 16)

            @blk2.tensor
            def _(t):
                entry_waits(t)
                t.wait_ge(s_yt, 16 * 8)
                t.wait_ge(s_rw, 32)
                for tsg in range(NTS):
                    if tsg >= 1:
                        t.wait_ge(s_lge, tsg)
                    ins = None
                    for dt in range(8):
                        ins = t.matmul(PL,
                                       YT[:, dt * TQ + tsg * 128:dt * TQ + (tsg + 1) * 128],
                                       GWB[:, dt * NE:(dt + 1) * NE],
                                       start=(dt == 0), stop=(dt == 7))
                    ins.then_inc(s_lgm, 1)
                t.wait_ge(s_lge, NTS)
                for tf in range(NTF):
                    for e in range(NE):
                        fe = tf * NE + e
                        for ns in range(NNS):
                            u = fe * NNS + ns
                            t.wait_ge(s_w1s[ns % 2], 16 * (u // 2 + 1))
                            if u >= 2:
                                t.wait_ge(s_fsil, u - 1)
                            ins = None
                            for dt in range(8):
                                ins = t.matmul(PUP[ns % 2][:],
                                               W1R[ns % 2][:, dt * 128:(dt + 1) * 128],
                                               YT[:, dt * TQ + tf * TF:dt * TQ + (tf + 1) * TF],
                                               start=(dt == 0), stop=(dt == 7))
                            ins.then_inc(s_fupm, 1)
                            if u >= 2:
                                t.wait_ge(s_fhm, u - 1)
                            t.wait_ge(s_w2s[ns % 2], 16 * (u // 2 + 1))
                            ins = None
                            for dt in range(8):
                                ins = t.matmul(PGT[ns % 2][:],
                                               W2R[ns % 2][:, dt * 128:(dt + 1) * 128],
                                               YT[:, dt * TQ + tf * TF:dt * TQ + (tf + 1) * TF],
                                               start=(dt == 0), stop=(dt == 7))
                            ins.then_inc(s_fupg, 1)
                        for mh in range(2):
                            if fe * 2 + mh >= 1:
                                t.wait_ge(s_fsc, (fe * 2 + mh) * NTSF)
                            for ns in range(NNS):
                                j = (fe * 2 + mh) * NNS + ns
                                t.wait_ge(s_f3s[j % 4], 16 * (j // 4 + 1))
                                if mh == 0:
                                    t.wait_ge(s_fhm, fe * NNS + ns + 1)
                                ins = None
                                for tsub in range(NTSF):
                                    ins = t.matmul(
                                        POUT[:, tsub * 512:(tsub + 1) * 512],
                                        HALL[:, ns * TF + tsub * 128:ns * TF + (tsub + 1) * 128],
                                        W3R[:, (j % 4) * D + mh * 512:(j % 4) * D + (mh + 1) * 512],
                                        start=(ns == 0), stop=(ns == NNS - 1))
                                ins.then_inc(s_f3c, 1)

            @blk2.scalar
            def _(s):
                entry_waits(s)
                gt.emit_range("scalar", s, "tsg0", "end")
                for tf in range(NTF):
                    if tf >= 1:
                        p4.emit_range("scalar", s, f"ts{(tf - 1) * NTSF}", f"out{tf * NTSF - 1}")
                    for e in range(NE):
                        fe = tf * NE + e
                        for ns in range(NNS):
                            u = fe * NNS + ns
                            s.wait_ge(s_fupm, u + 1)
                            if u >= 2:
                                s.wait_ge(s_fhm, u - 1)
                            s.activation(SU[ns % 2][:], PUP[ns % 2][:], AF.Silu).then_inc(s_fsil, 1)
                        for mh in range(2):
                            s.wait_ge(s_f3c, (fe * 2 + mh + 1) * NNS)
                            for tsub in range(NTSF):
                                i = (fe * 2 + mh) * NTSF + tsub
                                if i == 0:
                                    for smx, val in gt.milestone("end"):
                                        s.wait_ge(smx, val)
                                if i >= 2:
                                    s.wait_ge(s_fcm, i - 1)
                                tsg = tf * NTSF + tsub
                                s.activation(CT[i % 2][:], POUT[:, tsub * 512:(tsub + 1) * 512],
                                             AF.Identity,
                                             scale=WALL[:, tsg * NE + e:tsg * NE + e + 1]
                                             ).then_inc(s_fsc, 1)
                p4.emit_range("scalar", s, f"ts{(NTF - 1) * NTSF}", "end")

            @blk2.vector
            def _(v):
                entry_waits(v)
                vv2 = [0]
                base_vv = NGH

                def vs(ins):
                    vv2[0] += 1
                    ins.then_inc(s_vv, 1)
                    v.wait_ge(s_vv, base_vv + vv2[0])

                vs(v.memset(COMB[:], 0.0))
                for tsg in range(NTS):
                    v.wait_ge(s_lgm, tsg + 1)
                    ins = v.tensor_copy(LGA[:, tsg * NE:(tsg + 1) * NE], PL)
                    ins.then_inc(s_lge, 1)
                gt.emit_range("vector", v, "tsg0", "end")
                for tf in range(NTF):
                    if tf >= 1:
                        p4.emit_range("vector", v, f"ts{(tf - 1) * NTSF}", f"out{tf * NTSF - 1}")
                    for e in range(NE):
                        fe = tf * NE + e
                        for ns in range(NNS):
                            u = fe * NNS + ns
                            v.wait_ge(s_fupg, u + 1)
                            v.wait_ge(s_fsil, u + 1)
                            if fe >= 1:
                                v.wait_ge(s_f3c, (2 * fe - 1) * NNS + ns + 1)
                            v.tensor_tensor(HALL[:, ns * TF:(ns + 1) * TF],
                                            PGT[ns % 2][:], SU[ns % 2][:],
                                            ALU.mult).then_inc(s_fhm, 1)
                        for mh in range(2):
                            for tsub in range(NTSF):
                                i = (fe * 2 + mh) * NTSF + tsub
                                v.wait_ge(s_fsc, i + 1)
                                tsg = tf * NTSF + tsub
                                v.tensor_tensor(
                                    COMB[:, tsg * D + mh * 512:tsg * D + (mh + 1) * 512],
                                    COMB[:, tsg * D + mh * 512:tsg * D + (mh + 1) * 512],
                                    CT[i % 2][:], ALU.add).then_inc(s_fcm, 1)
                p4.emit_range("vector", v, f"ts{(NTF - 1) * NTSF}", "end")

    return nc


def prep_core_inputs(cfg, core, x, weight, gate_w, rms_w, W1, W2, W3):
    c = derived(cfg)
    T, TQ, DE = c["T"], c["TQ"], c["DE"]
    NNS = c["NNS"]
    b, q = divmod(core, 4)
    t0 = q * TQ
    xb = np.ascontiguousarray(np.roll(x[b], -t0, axis=0)).astype(np.float32, copy=False)
    return {
        "xb": xb,
        "wat": weight,
        "w1t": W1,
        "w2t": W2,
        "w3t": W3,
        "gwt": gate_w,
        "rmswb": rms_w,
    }


def prep_shared(cfg, weight, gate_w, rms_w, W1, W2, W3):
    c = derived(cfg)
    DE, NNS = c["DE"], c["NNS"]
    wat = np.ascontiguousarray(weight, dtype=np.float32)
    w1t = np.ascontiguousarray(
        W1.transpose(0, 2, 1).reshape(NE, D, NNS, 128).transpose(0, 2, 1, 3), dtype=np.float32)
    w2t = np.ascontiguousarray(
        W2.transpose(0, 2, 1).reshape(NE, D, NNS, 128).transpose(0, 2, 1, 3), dtype=np.float32)
    w3t = np.ascontiguousarray(W3.transpose(0, 2, 1), dtype=np.float32)
    gwt = np.ascontiguousarray(gate_w.T, dtype=np.float32)
    rw = np.ascontiguousarray(np.broadcast_to(np.asarray(rms_w, dtype=np.float32)[None, :],
                                              (128, D)))
    return wat, gwt, rw, w1t, w2t, w3t


EYE_NP = np.eye(128, dtype=np.float32)


def numpy_core_reference(cfg, xb, weight, gate_w, rms_w, W1, W2, W3):
    c = derived(cfg)
    T, TQ, DE = c["T"], c["TQ"], c["DE"]
    x = xb.astype(np.float64)

    def ln(a, eps=LN_EPS):
        m = a.mean(-1, keepdims=True)
        v = a.var(-1, keepdims=True)
        return (a - m) / np.sqrt(v + eps)

    h1 = ln(x)
    o = np.zeros((TQ, NH * D))
    for h in range(NH):
        yq = h1[:TQ, h * 64:(h + 1) * 64]
        yk = h1[:, h * 64:(h + 1) * 64]
        s = yq @ yk.T / 8.0
        a = np.exp(s - s.max(-1, keepdims=True))
        a = a / a.sum(-1, keepdims=True)
        o[:, h * D:(h + 1) * D] = a @ h1
    attn = o @ weight.astype(np.float64)
    x2 = x[:TQ] + ln(attn)
    h2 = ln(x2)
    y = h2 / np.sqrt((h2 ** 2).mean(-1, keepdims=True) + RMS_EPS) * rms_w.astype(np.float64)
    logits = y @ gate_w.astype(np.float64).T
    idx = np.argsort(-logits, axis=-1)[:, :2]
    vals = np.take_along_axis(logits, idx, axis=-1)
    w = np.exp(vals - vals.max(-1, keepdims=True))
    w = w / w.sum(-1, keepdims=True)
    wfull = np.zeros_like(logits)
    np.put_along_axis(wfull, idx, w, axis=-1)
    comb = np.zeros((TQ, D))
    for e in range(NE):
        up = y @ W1[e].astype(np.float64).T
        gate = y @ W2[e].astype(np.float64).T
        hh = up / (1 + np.exp(-up)) * gate
        oe = hh @ W3[e].astype(np.float64).T
        comb += oe * wfull[:, e:e + 1]
    ffn = h2 + comb
    return x2 + ln(ffn)


_NC_CACHE = {}


def _get_program():
    key = "full"
    if key not in _NC_CACHE:
        _NC_CACHE[key] = build_program(FULL)
    return _NC_CACHE[key]


def kernel(L0_attn_x, L0_attn_weight, L0_ffn_gate_w, L0_ffn_rms_w,
           L0_ffn_W1, L0_ffn_W2, L0_ffn_W3):
    from concourse.bass_utils import run_bass_kernel_spmd
    cfg = FULL
    c = derived(cfg)
    T, TQ = c["T"], c["TQ"]
    x = np.asarray(L0_attn_x, dtype=np.float32)
    wat, gwt, rw, w1t, w2t, w3t = prep_shared(
        cfg, np.asarray(L0_attn_weight, dtype=np.float32),
        np.asarray(L0_ffn_gate_w, dtype=np.float32),
        np.asarray(L0_ffn_rms_w, dtype=np.float32),
        np.asarray(L0_ffn_W1, dtype=np.float32),
        np.asarray(L0_ffn_W2, dtype=np.float32),
        np.asarray(L0_ffn_W3, dtype=np.float32))
    in_maps = []
    for core in range(8):
        b, q = divmod(core, 4)
        xb = np.ascontiguousarray(np.roll(x[b], -q * TQ, axis=0))
        in_maps.append({"xb": xb, "wat": wat, "w1t": w1t, "w2t": w2t,
                        "w3t": w3t, "gwt": gwt, "rmswb": rw, "eye": EYE_NP})
    nc = _get_program()
    res = run_bass_kernel_spmd(nc, in_maps, core_ids=list(range(8)))
    out = np.empty((2, T, D), dtype=np.float32)
    for core in range(8):
        b, q = divmod(core, 4)
        out[b, q * TQ:(q + 1) * TQ] = np.asarray(res.results[core]["out"])
    return out


# version 06
# speedup vs baseline: 1.0349x; 1.0349x over previous
import numpy as np
import contextlib
import concourse.bass as bass
import concourse.mybir as mybir

F32 = mybir.dt.float32
BF16 = mybir.dt.bfloat16
AF = mybir.ActivationFunctionType
ALU = mybir.AluOpType
AX = mybir.AxisListType

FULL = dict(T=8192, TQ=2048, DE=2816, TF=512)
MINI = dict(T=1024, TQ=256, DE=256, TF=256)

D = 1024
NH = 16
NE = 8
TC = 256
EXP_SCALE = 0.125
EXP_BIAS = -8.0
LN_EPS = 1e-5
RMS_EPS = 1e-6


def derived(cfg):
    c = dict(cfg)
    c["NTC"] = c["TQ"] // TC
    c["NSCH"] = c["T"] // 256
    c["SRES"] = c["T"] // 2
    c["NRES"] = c["SRES"] // 256
    c["NVG"] = (c["T"] - c["SRES"]) // 256
    c["NNS"] = c["DE"] // 128
    c["NTF"] = c["TQ"] // c["TF"]
    c["NTSF"] = c["TF"] // 128
    c["NTS"] = c["TQ"] // 128
    c["NGH"] = c["NTC"] * NH
    c["NST"] = c["T"] // 128
    return c


class Chain:

    def __init__(self, sems):
        self.sems = sems
        self.steps = []
        self.marks = {}

    def step(self, engine, fn, extra=()):
        self.steps.append((engine, fn, list(extra)))

    def mark(self, key):
        self.marks[key] = len(self.steps)

    def _count(self, idx, engine):
        return sum(1 for e, _, _ in self.steps[:idx] if e == engine)

    def milestone(self, key):
        idx = self.marks[key]
        return [(self.sems[e], self._count(idx, e)) for e in self.sems]

    def emit_range(self, engine, eng, lo_mark, hi_mark):
        lo = self.marks[lo_mark]
        hi = self.marks[hi_mark]
        for i in range(lo, hi):
            e, fn, extra = self.steps[i]
            if e != engine:
                continue
            for semx, val in extra:
                if val > 0:
                    eng.wait_ge(semx, val)
            for ename, semx in self.sems.items():
                vcnt = self._count(i, ename)
                if vcnt > 0:
                    eng.wait_ge(semx, vcnt)
            fn(eng).then_inc(self.sems[e], 1)


def build_program(cfg, phase_mode="all", no_vdma=False, no_ktw=False):
    c = derived(cfg)
    T, TQ, DE, TF = c["T"], c["TQ"], c["DE"], c["TF"]
    NTC, NSCH, SRES, NRES, NVG = c["NTC"], c["NSCH"], c["SRES"], c["NRES"], c["NVG"]
    NNS, NTF, NTSF, NTS, NGH, NST = c["NNS"], c["NTF"], c["NTSF"], c["NTS"], c["NGH"], c["NST"]
    NRT = SRES // 128

    nc = bass.Bass("TRN2", target_bir_lowering=False)

    xb = nc.dram_tensor("xb", [T, D], F32, kind="ExternalInput").ap()
    wat = nc.dram_tensor("wat", [NH * D, D], F32, kind="ExternalInput").ap()
    w1t = nc.dram_tensor("w1t", [NE, NNS, D, 128], F32, kind="ExternalInput").ap()
    w2t = nc.dram_tensor("w2t", [NE, NNS, D, 128], F32, kind="ExternalInput").ap()
    w3t = nc.dram_tensor("w3t", [NE, DE, D], F32, kind="ExternalInput").ap()
    gwt = nc.dram_tensor("gwt", [D, NE], F32, kind="ExternalInput").ap()
    rmswb = nc.dram_tensor("rmswb", [128, D], F32, kind="ExternalInput").ap()
    eye = nc.dram_tensor("eye", [128, 128], F32, kind="ExternalInput").ap()
    out = nc.dram_tensor("out", [TQ, D], F32, kind="ExternalOutput").ap()

    h1s = nc.dram_tensor("h1s", [T, D], BF16, kind="Internal").ap()
    h1d = nc.dram_tensor("h1d", [8, T, 128], BF16, kind="Internal").ap()
    h1t = nc.dram_tensor("h1t", [D, T], BF16, kind="Internal").ap()
    watb = nc.dram_tensor("watb", [NH * D, D], BF16, kind="Internal").ap()
    w1b = nc.dram_tensor("w1b", [NE, NNS, D, 128], BF16, kind="Internal").ap()
    w2b = nc.dram_tensor("w2b", [NE, NNS, D, 128], BF16, kind="Internal").ap()
    w3b = nc.dram_tensor("w3b", [NE, DE, D], BF16, kind="Internal").ap()
    gwb = nc.dram_tensor("gwb", [D, NE], BF16, kind="Internal").ap()
    x2s = nc.dram_tensor("x2s", [TQ, D], F32, kind="Internal").ap()
    yd = nc.dram_tensor("yd", [8, TQ, 128], BF16, kind="Internal").ap()

    es = contextlib.ExitStack()
    with es:
        def sem(name):
            return es.enter_context(nc.semaphore(name))

        s_x = [sem("s_x0"), sem("s_x1")]
        s_h1w = sem("s_h1w"); s_hw = [sem("s_hw0"), sem("s_hw1")]
        s_tr = sem("s_tr"); s_h1t = sem("s_h1t")
        s_kt = [sem("s_kt0"), sem("s_kt1")]
        s_w = sem("s_w"); s_v = [sem(f"s_v{i}") for i in range(4)]
        s_xr = [sem("s_xr0"), sem("s_xr1")]
        s_x2w = sem("s_x2w"); s_yw = sem("s_yw"); s_cw = sem("s_cw"); s_cf = sem("s_cf")
        s_yt = sem("s_yt")
        s_w1s = [sem("s_w1a"), sem("s_w1b")]; s_w2s = [sem("s_w2a"), sem("s_w2b")]
        s_f3s = [sem(f"s_f3{i}") for i in range(8)]
        s_x4 = [sem("s_x4a"), sem("s_x4b")]
        s_out = sem("s_out"); s_rw = sem("s_rw"); s_eye = sem("s_eye")
        s_smm = sem("s_smm"); s_exp = sem("s_exp"); s_vd = sem("s_vd")
        s_potev = sem("s_potev"); s_pjmm = sem("s_pjmm"); s_pjev = sem("s_pjev")
        s_dns = sem("s_dns"); s_trd = sem("s_trd"); s_rt = sem("s_rt")
        s_p1s = sem("s_p1s"); s_p1v = sem("s_p1v")
        s_ts = sem("s_ts"); s_tv = sem("s_tv")
        s_gs = sem("s_gs"); s_gv = sem("s_gv")
        s_4s = sem("s_4s"); s_4v = sem("s_4v")
        s_vv = sem("s_vv"); s_one = sem("s_one")
        s_lgm = sem("s_lgm"); s_lge = sem("s_lge")
        s_fupm = sem("s_fupm"); s_fupg = sem("s_fupg"); s_fsil = sem("s_fsil")
        s_fhm = sem("s_fhm"); s_fsc = sem("s_fsc"); s_fcm = sem("s_fcm")
        s_f3c = sem("s_f3c")

        ONES = es.enter_context(nc.sbuf_tensor("ONES", [128, 128], BF16))
        EYE = es.enter_context(nc.sbuf_tensor("EYE", [128, 128], F32))
        EBIA = es.enter_context(nc.sbuf_tensor("EBIA", [128, 1], F32))
        DUMP = es.enter_context(nc.sbuf_tensor("DUMP", [128, D], BF16))
        RMSWB = es.enter_context(nc.sbuf_tensor("RMSWB", [128, D], F32))

        s_p1s2 = sem("s_p1s2"); s_p1v2 = sem("s_p1v2")
        p1 = Chain({"scalar": s_p1s, "vector": s_p1v})
        p1b = Chain({"scalar": s_p1s2, "vector": s_p1v2})
        tl = Chain({"scalar": s_ts, "vector": s_tv})
        gt = Chain({"scalar": s_gs, "vector": s_gv})
        p4 = Chain({"scalar": s_4s, "vector": s_4v})

        a_es = contextlib.ExitStack()
        with a_es:
            def sb(name, shape, dt):
                return a_es.enter_context(nc.sbuf_tensor(name, shape, dt))
            H1R = sb("H1R", [128, NRT * D], BF16)
            XST = [sb(f"XST{i}", [128, D], F32) for i in range(2)]
            H1ST = [sb(f"H1ST{i}", [128, D], BF16) for i in range(2)]
            TR = sb("TR", [128, T], BF16)
            KTB = [sb(f"KTB{i}", [64, T], BF16) for i in range(2)]
            WBUF = sb("WBUF", [128, 8 * D], BF16)
            VR = sb("VR", [128, 8 * D], BF16)
            EB = sb("EB", [128, 1024], BF16)
            OTS = sb("OTS", [128, 8 * TC], BF16)
            DNS = sb("DNS", [128, TC], F32)
            RECT = [sb(f"RECT{i}", [128, 2], F32) for i in range(2)]
            PJA = [sb(f"PJA{i}", [128, 2 * D], F32) for i in range(2)]
            XO = [sb(f"XO{i}", [128, D], F32) for i in range(2)]
            X2T = sb("X2T", [128, D], F32)
            H2T = sb("H2T", [128, D], F32)
            YTL = sb("YTL", [128, D], BF16)
            stnames = ("S1", "S2", "MEAN", "VEPS", "R0", "Y0", "T1", "T3", "RSTD", "BIAS")
            STT = {k: sb("st" + k, [128, 1], F32) for k in stnames}
            STB = {k: sb("sb" + k, [128, 1], F32) for k in stnames}
            PS = a_es.enter_context(nc.psum_tensor("PS", [128, 1024], F32))
            POT = a_es.enter_context(nc.psum_tensor("POT", [128, 8 * TC], F32))
            PDN = a_es.enter_context(nc.psum_tensor("PDN", [128, TC], F32))
            PPJ = a_es.enter_context(nc.psum_tensor("PPJ", [128, 512], F32))

            def h1dst(st):
                return H1R[:, st * D:(st + 1) * D] if st < NRT else H1ST[st % 2][:]

            def v_tile(gh, k, st2):
                stile = 2 * k + st2
                if no_vdma:
                    stile = stile % NRT
                if stile < NRT:
                    return H1R[:, stile * D:(stile + 1) * D]
                off = stile - NRT
                G = gh * NVG + off // 2
                slot = G % 4
                return VR[:, (slot * 2 + off % 2) * D:(slot * 2 + off % 2 + 1) * D]

            def ln_stats(ch, ST, src_tile, eps, with_mean, extra_first=(), tag=None,
                         sq_vector=False, dump=None):
                S1, S2, MEAN, VEPS, R0, Y0, T1, T3, RSTD, BIAS = (ST[k] for k in stnames)
                if tag:
                    ch.mark(tag + ":0")
                if sq_vector:
                    ch.step("vector", lambda v, src=src_tile: v.tensor_mul(
                        DUMP[:], src, src), extra_first)
                    ch.step("vector", lambda v: v.reduce_sum(S2[:], DUMP[:], axis=AX.X))
                else:
                    dmp = DUMP[:] if dump is None else dump
                    ch.step("scalar", lambda s, src=src_tile, dmp=dmp: s.activation(
                        dmp, src, AF.Square, accum_out=S2[:]), extra_first)
                if tag:
                    ch.mark(tag + ":1")
                if with_mean:
                    ch.step("vector", lambda v, src=src_tile: v.reduce_sum(S1[:], src, axis=AX.X))
                    ch.step("vector", lambda v: v.tensor_scalar(MEAN[:], S1[:], 1.0 / D, None, ALU.mult))
                    ch.step("vector", lambda v: v.tensor_mul(T1[:], MEAN[:], MEAN[:]))
                    ch.step("vector", lambda v: v.tensor_scalar(VEPS[:], S2[:], 1.0 / D, None, ALU.mult))
                    ch.step("vector", lambda v: v.tensor_sub(VEPS[:], VEPS[:], T1[:]))
                    ch.step("vector", lambda v, e=eps: v.tensor_scalar(VEPS[:], VEPS[:], e, None, ALU.add))
                else:
                    ch.step("vector", lambda v, e=eps: v.tensor_scalar(
                        VEPS[:], S2[:], 1.0 / D, e, ALU.mult, ALU.add))
                ch.step("vector", lambda v: v.reciprocal(R0[:], VEPS[:]))
                if tag:
                    ch.mark(tag + ":2")
                ch.step("scalar", lambda s: s.activation(Y0[:], R0[:], AF.Sqrt))
                if tag:
                    ch.mark(tag + ":3")
                ch.step("vector", lambda v: v.tensor_mul(T1[:], Y0[:], Y0[:]))
                ch.step("vector", lambda v: v.tensor_mul(T1[:], T1[:], VEPS[:]))
                ch.step("vector", lambda v: v.tensor_scalar(T3[:], T1[:], -0.5, 1.5, ALU.mult, ALU.add))
                ch.step("vector", lambda v: v.tensor_mul(RSTD[:], Y0[:], T3[:]))
                if with_mean:
                    ch.step("vector", lambda v: v.scalar_tensor_tensor(
                        BIAS[:], MEAN[:], -1.0, RSTD[:], ALU.mult, ALU.mult))

            def p1_chain(st):
                return p1 if st % 2 == 0 else p1b

            p1.mark("start"); p1b.mark("start")
            for st in range(NST):
                ch = p1_chain(st)
                ST = STT if st % 2 == 0 else STB
                ch.mark(f"t{st}")
                ln_stats(ch, ST, XST[st % 2][:], LN_EPS, True,
                         [(s_x[st % 2], 16 * (st // 2 + 1))], tag=f"p{st}",
                         dump=POT[:, (st % 2) * 1024:(st % 2) * 1024 + 1024])
                h1_extra = ([(s_hw[st % 2], 32 * ((st - NRT) // 2))]
                            if st >= NRT + 2 else [])
                ch.step("scalar", lambda s, st=st, ST=ST: s.activation(
                    h1dst(st), XST[st % 2][:], AF.Identity,
                    bias=ST["BIAS"][:], scale=ST["RSTD"][:]), h1_extra)
                ch.mark(f"h1_{st}")
            p1.mark("end"); p1b.mark("end")

            def p1_milestone(st):
                return p1_chain(st).milestone(f"h1_{st}")

            for tc in range(NTC):
                for tsub in range(2):
                    ts = tc * 2 + tsub
                    tl.mark(f"ts{ts}")
                    PA = PJA[tc % 2][:, tsub * D:(tsub + 1) * D]
                    ln_stats(tl, STT, PA, LN_EPS, True, [(s_pjev, 4 * NH * (tc + 1))],
                             sq_vector=True)
                    x2_extra = [(s_xr[ts % 2], 16 * (ts // 2 + 1))]
                    if ts >= 1:
                        x2_extra.append((s_x2w, 16 * ts))
                    tl.step("vector", lambda v, ts=ts, PA=PA: v.scalar_tensor_tensor(
                        X2T[:], PA, STT["RSTD"][:], XO[ts % 2][:], ALU.mult, ALU.add), x2_extra)
                    tl.mark(f"pa_done{ts}")
                    tl.step("vector", lambda v: v.tensor_scalar(
                        X2T[:], X2T[:], STT["BIAS"][:], None, ALU.add))
                    tl.mark(f"x2_{ts}")
                    ln_stats(tl, STT, X2T[:], LN_EPS, True, sq_vector=True)
                    tl.step("vector", lambda v: v.tensor_scalar(
                        H2T[:], X2T[:], STT["RSTD"][:], STT["BIAS"][:], ALU.mult, ALU.add))
                    tl.mark(f"x2free{ts}")
                    ln_stats(tl, STT, H2T[:], RMS_EPS, False, sq_vector=True)
                    y_extra = [(s_rw, 16)]
                    if ts >= 1:
                        y_extra.append((s_yw, 16 * ts))
                    tl.step("vector", lambda v: v.scalar_tensor_tensor(
                        YTL[:], H2T[:], STT["RSTD"][:], RMSWB[:], ALU.mult, ALU.mult), y_extra)
                    tl.mark(f"y_{ts}")
            tl.mark("end")

            blk = a_es.enter_context(nc.Block())

            @blk.sync
            def _(sy):
                sy.dma_start(out=RMSWB[:], in_=rmswb).then_inc(s_rw, 16)
                sy.dma_start(out=EYE[:], in_=eye).then_inc(s_eye, 16)
                for st in range(NST):
                    if st >= 2:
                        for smx, val in p1_milestone(st - 2):
                            sy.wait_ge(smx, val)
                    sy.dma_start(out=XST[st % 2][:],
                                 in_=xb[st * 128:(st + 1) * 128, :]).then_inc(s_x[st % 2], 16)
                sy.wait_ge(s_h1w, 32 * NRT)
                nhw0 = sum(1 for st in range(NRT, NST) if st % 2 == 0)
                nhw1 = (NST - NRT) - nhw0
                if nhw0:
                    sy.wait_ge(s_hw[0], 32 * nhw0)
                if nhw1:
                    sy.wait_ge(s_hw[1], 32 * nhw1)
                for dt in range(8):
                    if dt >= 1:
                        sy.wait_ge(s_h1t, 16 * dt)
                    sy.dma_start_transpose(out=TR[:], in_=h1d[dt]).then_inc(s_tr, 16)
                    sy.wait_ge(s_tr, 16 * (dt + 1))
                    sy.dma_start(out=h1t[dt * 128:(dt + 1) * 128, :], in_=TR[:]).then_inc(s_h1t, 16)
                def xo_loads(sy, tcx):
                    for tsub in range(2):
                        ts = tcx * 2 + tsub
                        if ts >= 2:
                            for smx, val in tl.milestone(f"x2_{ts - 2}"):
                                sy.wait_ge(smx, val)
                        sy.dma_start(out=XO[ts % 2][:],
                                     in_=xb[ts * 128:(ts + 1) * 128, :]).then_inc(s_xr[ts % 2], 16)

                def kt_load(sy, gh):
                    if gh >= NGH:
                        return
                    hh_ = gh % NH
                    if gh >= 2:
                        sy.wait_ge(s_smm, (gh - 1) * NSCH)
                    sy.wait_ge(s_h1t, 16 * (hh_ // 2 + 1))
                    sy.dma_start(out=KTB[gh % 2][0:64, :],
                                 in_=h1t[hh_ * 64:(hh_ + 1) * 64, :]).then_inc(s_kt[gh % 2], 16)

                kt_load(sy, 0)
                for tc in range(NTC):
                    if tc >= 1:
                        xo_loads(sy, tc - 1)
                    for h in range(NH):
                        gh = tc * NH + h
                        kt_load(sy, gh + 1)
                        for g in range(0 if no_vdma else NVG):
                            G = gh * NVG + g
                            if G >= 4:
                                ghp, gp = divmod(G - 4, NVG)
                                sy.wait_ge(s_vd, ghp * NSCH + NRES + gp + 1)
                            r0 = SRES + g * 256
                            sy.dma_start(
                                out=VR[:, ((G % 4) * 2) * D:((G % 4) * 2 + 2) * D]
                                .rearrange("p (t d) -> p t d", d=D),
                                in_=h1s[r0:r0 + 256, :].rearrange("(t p) d -> p t d", p=128)
                            ).then_inc(s_v[G % 4], 16)
                        if gh >= 1:
                            sy.wait_ge(s_pjmm, 4 * gh)
                        if tc == 0:
                            sy.wait_ge(s_cw, 16 * (h + 1))
                        sy.dma_start(out=WBUF[:].rearrange("p (c d) -> p c d", d=D),
                                     in_=watb[h * D:(h + 1) * D, :]
                                     .rearrange("(c p) d -> p c d", p=128)).then_inc(s_w, 16)
                xo_loads(sy, NTC - 1)

            @blk.gpsimd
            def _(g):
                for st in range(NST):
                    for smx, val in p1_milestone(st):
                        g.wait_ge(smx, val)
                    src = h1dst(st)
                    dsem = s_h1w if st < NRT else s_hw[st % 2]
                    g.dma_start(out=h1s[st * 128:(st + 1) * 128, :], in_=src).then_inc(dsem, 16)
                    g.dma_start(out=h1d[:, st * 128:(st + 1) * 128, :].rearrange("dt p d -> p dt d"),
                                in_=src.rearrange("p (dt d) -> p dt d", d=128)).then_inc(dsem, 16)
                for ci in range(NH):
                    if ci >= 1:
                        g.wait_ge(s_cw, 16 * ci)
                    g.dma_start(out=watb[ci * D:(ci + 1) * D, :],
                                in_=wat[ci * D:(ci + 1) * D, :]).then_inc(s_cw, 16)
                g.dma_start(out=gwb, in_=gwt).then_inc(s_cf, 16)
                for e in range(NE):
                    g.dma_start(out=w1b[e], in_=w1t[e]).then_inc(s_cf, 16)
                    g.dma_start(out=w2b[e], in_=w2t[e]).then_inc(s_cf, 16)
                    g.dma_start(out=w3b[e], in_=w3t[e]).then_inc(s_cf, 16)
                for ts in range(2 * NTC):
                    for smx, val in tl.milestone(f"x2free{ts}"):
                        g.wait_ge(smx, val)
                    g.dma_start(out=x2s[ts * 128:(ts + 1) * 128, :], in_=X2T[:]).then_inc(s_x2w, 16)
                    for smx, val in tl.milestone(f"y_{ts}"):
                        g.wait_ge(smx, val)
                    g.dma_start(out=yd[:, ts * 128:(ts + 1) * 128, :].rearrange("dt p d -> p dt d"),
                                in_=YTL[:].rearrange("p (dt d) -> p dt d", d=128)).then_inc(s_yw, 16)

            @blk.tensor
            def _(t):
                t.wait_ge(s_one, 2)
                t.wait_ge(s_eye, 16)

                def emit_S(gh, tc, k):
                    sg = gh * NSCH + k
                    if k == 0:
                        t.wait_ge(s_kt[gh % 2], 16 * (gh // 2 + 1))
                    if sg >= 2:
                        t.wait_ge(s_exp, sg - 1)
                    KT = KTB[gh % 2]
                    base = (k % 2) * 512
                    t.matmul(PS[:, base:base + TC],
                             KT[0:64, (2 * k) * 128:(2 * k + 1) * 128],
                             KT[0:64, tc * TC:(tc + 1) * TC], start=True, stop=True)
                    t.matmul(PS[:, base + TC:base + 512],
                             KT[0:64, (2 * k + 1) * 128:(2 * k + 2) * 128],
                             KT[0:64, tc * TC:(tc + 1) * TC], start=True, stop=True).then_inc(s_smm, 1)

                def emit_V(gh, k):
                    sg = gh * NSCH + k
                    t.wait_ge(s_exp, sg + 1)
                    if k == 0 and gh >= 1:
                        t.wait_ge(s_potev, 2 * gh)
                        t.wait_ge(s_dns, gh)
                    if k >= NRES and not no_vdma:
                        G = gh * NVG + (k - NRES)
                        t.wait_ge(s_v[G % 4], 16 * (G // 4 + 1))
                    last_k = (k == NSCH - 1)
                    ins = None
                    for st2 in range(2):
                        rhs = EB[:, ((k % 2) * 2 + st2) * TC:((k % 2) * 2 + st2 + 1) * TC]
                        first = (k == 0 and st2 == 0)
                        last = (last_k and st2 == 1)
                        t.matmul(PDN[:], ONES[:], rhs, start=first, stop=last)
                        vt = v_tile(gh, k, st2)
                        for cc in range(8):
                            ins = t.matmul(POT[:, cc * TC:(cc + 1) * TC],
                                           vt[:, cc * 128:(cc + 1) * 128], rhs,
                                           start=(first and cc % 2 == 0),
                                           stop=(last and cc % 2 == 1))
                    ins.then_inc(s_vd, 1)

                def emit_pq(pgh, q):
                    tsub, dh = divmod(q, 2)
                    t.wait_ge(s_potev, 2 * (pgh + 1))
                    t.wait_ge(s_w, 16 * (pgh + 1))
                    t.wait_ge(s_pjev, pgh * 4 + q)
                    if q == 0:
                        t.wait_ge(s_rt, pgh + 1)
                    ins = None
                    for cc in range(8):
                        ins = t.matmul(PPJ[:],
                                       OTS[:, cc * TC + tsub * 128:cc * TC + (tsub + 1) * 128],
                                       WBUF[:, cc * D + dh * 512:cc * D + (dh + 1) * 512],
                                       start=(cc == 0), stop=(cc == 7))
                    ins.then_inc(s_pjmm, 1)

                pos2 = {}
                for q in range(4):
                    kq = max(1, min(NSCH - 1, (q + 1) * NSCH // 5))
                    pos2.setdefault(kq, []).append(q)

                def emit_trd(pgh):
                    t.wait_ge(s_dns, pgh + 1)
                    t.wait_ge(s_pjev, 4 * pgh)
                    t.transpose(PPJ[:, 0:128], DNS[:, 0:128], EYE[:])
                    t.transpose(PPJ[:, 128:256], DNS[:, 128:256], EYE[:]).then_inc(s_trd, 1)

                for tc in range(NTC):
                    for h in range(NH):
                        gh = tc * NH + h
                        emit_S(gh, tc, 0)
                        for k in range(1, NSCH):
                            emit_S(gh, tc, k)
                            if k == 1 and gh >= 1:
                                emit_trd(gh - 1)
                            if gh >= 1:
                                for q in pos2.get(k, ()):
                                    emit_pq(gh - 1, q)
                            emit_V(gh, k - 1)
                        emit_V(gh, NSCH - 1)
                emit_trd(NGH - 1)
                for q in range(4):
                    emit_pq(NGH - 1, q)

            @blk.scalar
            def _(s):
                s.wait_ge(s_one, 2)
                for st0 in range(0, NST, 2):
                    for seg in ((f"p{{}}:0", f"p{{}}:1"), (f"p{{}}:2", f"p{{}}:3"),
                                (f"p{{}}:3", "h1_{}")):
                        for st in (st0, st0 + 1):
                            lo = seg[0].format(st)
                            hi = seg[1].format(st) if "h1" not in seg[1] else f"h1_{st}"
                            p1_chain(st).emit_range("scalar", s, lo, hi)
                for tc in range(NTC):
                    for h in range(NH):
                        gh = tc * NH + h
                        for k in range(NSCH):
                            sg = gh * NSCH + k
                            s.wait_ge(s_smm, sg + 1)
                            if sg >= 2:
                                s.wait_ge(s_vd, sg - 1)
                            s.activation(EB[:, (k % 2) * 512:(k % 2) * 512 + 512],
                                         PS[:, (k % 2) * 512:(k % 2) * 512 + 512],
                                         AF.Exp, scale=EXP_SCALE, bias=EBIA[:]).then_inc(s_exp, 1)
                        s.wait_ge(s_vd, (gh + 1) * NSCH)
                        if gh >= 1:
                            s.wait_ge(s_pjmm, 4 * gh)
                        s.activation(OTS[:, 0:4 * TC], POT[:, 0:4 * TC],
                                     AF.Identity).then_inc(s_potev, 1)
                        if h == 1 and tc >= 1:
                            tl.emit_range("scalar", s, f"ts{(tc - 1) * 2}", f"y_{(tc - 1) * 2 + 1}")
                tl.emit_range("scalar", s, f"ts{(NTC - 1) * 2}", "end")

            @blk.vector
            def _(v):
                vv = [0]

                def vsync(ins):
                    vv[0] += 1
                    ins.then_inc(s_vv, 1)
                    v.wait_ge(s_vv, vv[0])

                v.memset(ONES[:], 1.0).then_inc(s_one, 1)
                v.memset(EBIA[:], EXP_BIAS).then_inc(s_one, 1)
                for st0 in range(0, NST, 2):
                    for seg in (("p{}:0", "p{}:2"), ("p{}:3", "h1_{}")):
                        for st in (st0, st0 + 1):
                            lo = seg[0].format(st)
                            hi = seg[1].format(st)
                            p1_chain(st).emit_range("vector", v, lo, hi)

                def emit_dns(gh):
                    v.wait_ge(s_vd, (gh + 1) * NSCH)
                    v.tensor_copy(DNS[:], PDN[:]).then_inc(s_dns, 1)
                    if gh >= 1:
                        v.wait_ge(s_pjmm, 4 * gh)
                    v.tensor_copy(OTS[:, 4 * TC:8 * TC],
                                  POT[:, 4 * TC:8 * TC]).then_inc(s_potev, 1)

                def emit_recip2(pgh):
                    v.wait_ge(s_trd, pgh + 1)
                    vsync(v.reciprocal(RECT[pgh % 2][:, 0:1], PPJ[:, 0:1]))
                    ins = v.reciprocal(RECT[pgh % 2][:, 1:2], PPJ[:, 128:129])
                    ins.then_inc(s_rt, 1)
                    vv[0] += 0

                def emit_adds(pgh):
                    ptc = pgh // NH
                    for q in range(4):
                        tsub, dh = divmod(q, 2)
                        dst = PJA[ptc % 2][:, tsub * D + dh * 512:tsub * D + (dh + 1) * 512]
                        v.wait_ge(s_pjmm, pgh * 4 + q + 1)
                        rsc = RECT[pgh % 2][:, tsub:tsub + 1]
                        if pgh % NH == 0:
                            if ptc >= 2:
                                for smx, val in tl.milestone(f"pa_done{(ptc - 2) * 2 + 1}"):
                                    v.wait_ge(smx, val)
                            ins = v.tensor_scalar(dst, PPJ[:], rsc, None, ALU.mult)
                        else:
                            ins = v.scalar_tensor_tensor(dst, PPJ[:], rsc, dst, ALU.mult, ALU.add)
                        ins.then_inc(s_pjev, 1)

                for tc in range(NTC):
                    for h in range(NH):
                        gh = tc * NH + h
                        if gh >= 1:
                            emit_recip2(gh - 1)
                            emit_adds(gh - 1)
                        emit_dns(gh)
                        if h == 1 and tc >= 1:
                            tl.emit_range("vector", v, f"ts{(tc - 1) * 2}", f"y_{(tc - 1) * 2 + 1}")
                emit_recip2(NGH - 1)
                emit_adds(NGH - 1)
                tl.emit_range("vector", v, f"ts{(NTC - 1) * 2}", "end")

        if phase_mode == "attn":
            with nc.Block() as blk2s:
                @blk2s.sync
                def _(sy):
                    eng = sy
                    eng.wait_ge(s_yw, 16 * 2 * NTC)
                    eng.wait_ge(s_x2w, 16 * 2 * NTC)
                    for smx, val in tl.milestone("end"):
                        eng.wait_ge(smx, val)
                    for ts in range(NTS):
                        sy.dma_start(out=out[ts * 128:(ts + 1) * 128, :],
                                     in_=x2s[ts * 128:(ts + 1) * 128, :]).then_inc(s_out, 16)
                    sy.wait_ge(s_out, 16 * NTS)
            return nc

        f_es = contextlib.ExitStack()
        with f_es:
            def fb(name, shape, dt):
                return f_es.enter_context(nc.sbuf_tensor(name, shape, dt))
            YT = fb("YT", [128, 8 * TQ], BF16)
            HALL = fb("HALL", [128, NNS * TF], BF16)
            COMB = fb("COMB", [128, NTS * D], F32)
            W1R = [fb(f"W1R{i}", [128, 8 * 128], BF16) for i in range(2)]
            W2R = [fb(f"W2R{i}", [128, 8 * 128], BF16) for i in range(2)]
            W3R = fb("W3R", [128, 8 * D], BF16)
            SU = [fb(f"SU{i}", [128, TF], BF16) for i in range(2)]
            CT = [fb(f"CT{i}", [128, 512], F32) for i in range(2)]
            GWB = fb("GWB", [128, 8 * NE], BF16)
            LGA = fb("LGA", [128, NTS * NE], F32)
            WALL = fb("WALL", [128, NTS * NE], F32)
            X2R = [fb(f"X2R{i}", [128, D], F32) for i in range(2)]
            H2P = fb("H2P", [128, D], F32)
            FFT = fb("FFT", [128, D], F32)
            FLN = fb("FLN", [128, D], F32)
            OUTT = fb("OUTT", [128, D], F32)
            G_ = {k: fb("g" + k, [128, NE], F32) for k in ("MK1", "LM", "MK2", "EL", "TMP")}
            G1 = {k: fb("g1" + k, [128, 1], F32)
                  for k in ("M1", "M2", "NM1", "D21", "E2", "DN2", "RCW")}
            ST2 = {k: fb("s2" + k, [128, 1], F32) for k in stnames}
            PUP = [f_es.enter_context(nc.psum_tensor(f"PUP{i}", [128, TF], F32)) for i in range(2)]
            PGT = [f_es.enter_context(nc.psum_tensor(f"PGT{i}", [128, TF], F32)) for i in range(2)]
            POUT = f_es.enter_context(nc.psum_tensor("POUT", [128, NTSF * 512], F32))
            PL = POUT[:, 0:NE]

            NFC = 1 + 3 * NE

            def entry_waits(eng):
                eng.wait_ge(s_vd, NGH * NSCH)
                eng.wait_ge(s_exp, NGH * NSCH)
                eng.wait_ge(s_pjmm, 4 * NGH)
                eng.wait_ge(s_pjev, 4 * NGH)
                eng.wait_ge(s_potev, 2 * NGH)
                eng.wait_ge(s_dns, NGH)
                eng.wait_ge(s_trd, NGH)
                eng.wait_ge(s_rt, NGH)
                eng.wait_ge(s_yw, 16 * 2 * NTC)
                eng.wait_ge(s_x2w, 16 * 2 * NTC)
                eng.wait_ge(s_kt[0], 16 * ((NGH + 1) // 2))
                eng.wait_ge(s_kt[1], 16 * (NGH // 2))
                eng.wait_ge(s_w, 16 * NGH)
                ngv = 0 if no_vdma else NGH * NVG
                for si in range(4):
                    cnt = (ngv - si + 3) // 4
                    if cnt > 0:
                        eng.wait_ge(s_v[si], 16 * cnt)
                for smx, val in tl.milestone("end"):
                    eng.wait_ge(smx, val)

            for tsg in range(NTS):
                gt.mark(f"tsg{tsg}")
                LG = LGA[:, tsg * NE:(tsg + 1) * NE]
                WF = WALL[:, tsg * NE:(tsg + 1) * NE]
                gt.step("vector", lambda v, LG=LG: v.reduce_max(G1["M1"][:], LG, axis=AX.X),
                        [(s_lge, tsg + 1)])
                gt.step("vector", lambda v, LG=LG: v.tensor_scalar(
                    G_["MK1"][:], LG, G1["M1"][:], None, ALU.is_ge))
                gt.step("vector", lambda v, LG=LG: v.scalar_tensor_tensor(
                    G_["LM"][:], G_["MK1"][:], -1e30, LG, ALU.mult, ALU.add))
                gt.step("vector", lambda v: v.reduce_max(G1["M2"][:], G_["LM"][:], axis=AX.X))
                gt.step("vector", lambda v, LG=LG: v.tensor_scalar(
                    G_["MK2"][:], LG, G1["M2"][:], None, ALU.is_ge))
                gt.step("vector", lambda v: v.tensor_scalar_mul(G1["NM1"][:], G1["M1"][:], -1.0))
                gt.step("scalar", lambda s, LG=LG: s.activation(
                    G_["EL"][:], LG, AF.Exp, bias=G1["NM1"][:]))
                gt.step("vector", lambda v: v.tensor_sub(G1["D21"][:], G1["M2"][:], G1["M1"][:]))
                gt.step("scalar", lambda s: s.activation(G1["E2"][:], G1["D21"][:], AF.Exp))
                gt.step("vector", lambda v: v.tensor_scalar(
                    G1["DN2"][:], G1["E2"][:], 1.0, None, ALU.add))
                gt.step("vector", lambda v: v.reciprocal(G1["RCW"][:], G1["DN2"][:]))
                gt.step("vector", lambda v: v.tensor_scalar(
                    G_["TMP"][:], G_["EL"][:], G1["RCW"][:], None, ALU.mult))
                gt.step("vector", lambda v, WF=WF: v.tensor_mul(WF, G_["TMP"][:], G_["MK2"][:]))
            gt.mark("end")

            for ts in range(NTS):
                tf = ts // NTSF
                p4.mark(f"ts{ts}")
                ln_extra = [(s_x4[ts % 2], 16 * (ts // 2 + 1))]
                ln_stats(p4, ST2, X2R[ts % 2][:], LN_EPS, True, ln_extra, sq_vector=True)
                p4.step("vector", lambda v, ts=ts: v.tensor_scalar(
                    H2P[:], X2R[ts % 2][:], ST2["RSTD"][:], ST2["BIAS"][:], ALU.mult, ALU.add))
                p4.step("vector", lambda v, ts=ts: v.tensor_add(
                    FFT[:], H2P[:], COMB[:, ts * D:(ts + 1) * D]),
                    [(s_fcm, (tf + 1) * 2 * NE * NTSF)])
                ln_stats(p4, ST2, FFT[:], LN_EPS, True, sq_vector=True)
                p4.step("vector", lambda v: v.tensor_scalar(
                    FLN[:], FFT[:], ST2["RSTD"][:], ST2["BIAS"][:], ALU.mult, ALU.add))
                p4.step("vector", lambda v, ts=ts: v.tensor_add(OUTT[:], X2R[ts % 2][:], FLN[:]),
                        [(s_out, 16 * ts)] if ts >= 1 else [])
                p4.mark(f"out{ts}")
            p4.mark("end")

            blk2 = f_es.enter_context(nc.Block())

            @blk2.sync
            def _(sy):
                entry_waits(sy)
                sy.wait_ge(s_cf, 16 * NFC)
                for dt in range(8):
                    sy.dma_start_transpose(out=YT[:, dt * TQ:(dt + 1) * TQ],
                                           in_=yd[dt]).then_inc(s_yt, 16)
                sy.dma_start(out=GWB[:].rearrange("p (dt e) -> p dt e", e=NE),
                             in_=gwb.rearrange("(dt p) e -> p dt e", p=128)).then_inc(s_rw, 16)
                for tf in range(NTF):
                    for e in range(NE):
                        fe = tf * NE + e
                        for ns in range(NNS):
                            u = fe * NNS + ns
                            if u >= 2:
                                sy.wait_ge(s_fupm, u - 1)
                            sy.dma_start(out=W1R[ns % 2][:].rearrange("p (dt n) -> p dt n", n=128),
                                         in_=w1b[e, ns].rearrange("(dt p) n -> p dt n", p=128)
                                         ).then_inc(s_w1s[ns % 2], 16)
                            if u >= 2:
                                sy.wait_ge(s_fupg, u - 1)
                            sy.dma_start(out=W2R[ns % 2][:].rearrange("p (dt n) -> p dt n", n=128),
                                         in_=w2b[e, ns].rearrange("(dt p) n -> p dt n", p=128)
                                         ).then_inc(s_w2s[ns % 2], 16)
                        for mh in range(2):
                            for ns in range(NNS):
                                j = (fe * 2 + mh) * NNS + ns
                                if j >= 8:
                                    sy.wait_ge(s_f3c, j - 7)
                                sy.dma_start(out=W3R[:, (j % 8) * D:(j % 8 + 1) * D],
                                             in_=w3b[e, ns * 128:(ns + 1) * 128, :]
                                             ).then_inc(s_f3s[j % 8], 16)
                sy.wait_ge(s_out, 16 * NTS)

            @blk2.gpsimd
            def _(g):
                entry_waits(g)
                for ts in range(NTS):
                    if ts >= 2:
                        for smx, val in p4.milestone(f"out{ts - 2}"):
                            g.wait_ge(smx, val)
                    g.dma_start(out=X2R[ts % 2][:],
                                in_=x2s[ts * 128:(ts + 1) * 128, :]).then_inc(s_x4[ts % 2], 16)
                    if ts >= 1:
                        for smx, val in p4.milestone(f"out{ts - 1}"):
                            g.wait_ge(smx, val)
                        g.dma_start(out=out[(ts - 1) * 128:ts * 128, :],
                                    in_=OUTT[:]).then_inc(s_out, 16)
                for smx, val in p4.milestone(f"out{NTS - 1}"):
                    g.wait_ge(smx, val)
                g.dma_start(out=out[(NTS - 1) * 128:NTS * 128, :],
                            in_=OUTT[:]).then_inc(s_out, 16)

            @blk2.tensor
            def _(t):
                entry_waits(t)
                t.wait_ge(s_yt, 16 * 8)
                t.wait_ge(s_rw, 32)
                for tsg in range(NTS):
                    if tsg >= 1:
                        t.wait_ge(s_lge, tsg)
                    ins = None
                    for dt in range(8):
                        ins = t.matmul(PL,
                                       YT[:, dt * TQ + tsg * 128:dt * TQ + (tsg + 1) * 128],
                                       GWB[:, dt * NE:(dt + 1) * NE],
                                       start=(dt == 0), stop=(dt == 7))
                    ins.then_inc(s_lgm, 1)
                t.wait_ge(s_lge, NTS)
                for tf in range(NTF):
                    for e in range(NE):
                        fe = tf * NE + e
                        for ns in range(NNS):
                            u = fe * NNS + ns
                            t.wait_ge(s_w1s[ns % 2], 16 * (u // 2 + 1))
                            if u >= 2:
                                t.wait_ge(s_fsil, u - 1)
                            ins = None
                            for dt in range(8):
                                ins = t.matmul(PUP[ns % 2][:],
                                               W1R[ns % 2][:, dt * 128:(dt + 1) * 128],
                                               YT[:, dt * TQ + tf * TF:dt * TQ + (tf + 1) * TF],
                                               start=(dt == 0), stop=(dt == 7))
                            ins.then_inc(s_fupm, 1)
                            if u >= 2:
                                t.wait_ge(s_fhm, u - 1)
                            t.wait_ge(s_w2s[ns % 2], 16 * (u // 2 + 1))
                            ins = None
                            for dt in range(8):
                                ins = t.matmul(PGT[ns % 2][:],
                                               W2R[ns % 2][:, dt * 128:(dt + 1) * 128],
                                               YT[:, dt * TQ + tf * TF:dt * TQ + (tf + 1) * TF],
                                               start=(dt == 0), stop=(dt == 7))
                            ins.then_inc(s_fupg, 1)
                        for mh in range(2):
                            if fe * 2 + mh >= 1:
                                t.wait_ge(s_fsc, (fe * 2 + mh) * NTSF)
                            for ns in range(NNS):
                                j = (fe * 2 + mh) * NNS + ns
                                t.wait_ge(s_f3s[j % 8], 16 * (j // 8 + 1))
                                if mh == 0:
                                    t.wait_ge(s_fhm, fe * NNS + ns + 1)
                                ins = None
                                for tsub in range(NTSF):
                                    ins = t.matmul(
                                        POUT[:, tsub * 512:(tsub + 1) * 512],
                                        HALL[:, ns * TF + tsub * 128:ns * TF + (tsub + 1) * 128],
                                        W3R[:, (j % 8) * D + mh * 512:(j % 8) * D + (mh + 1) * 512],
                                        start=(ns == 0), stop=(ns == NNS - 1))
                                ins.then_inc(s_f3c, 1)

            @blk2.scalar
            def _(s):
                entry_waits(s)
                gt.emit_range("scalar", s, "tsg0", "end")

                def p4_seg(i, tfx):
                    ts = tfx * NTSF + i
                    lo = f"ts{ts}"
                    hi = f"out{ts}" if i < NTSF - 1 else f"out{ts}"
                    return lo, f"out{ts}"

                for tf in range(NTF):
                    for e in range(NE):
                        fe = tf * NE + e
                        for ns in range(NNS):
                            u = fe * NNS + ns
                            s.wait_ge(s_fupm, u + 1)
                            if u >= 2:
                                s.wait_ge(s_fhm, u - 1)
                            s.activation(SU[ns % 2][:], PUP[ns % 2][:], AF.Silu).then_inc(s_fsil, 1)
                        if tf >= 1 and e % 2 == 1 and e // 2 < NTSF:
                            ts_ = (tf - 1) * NTSF + e // 2
                            p4.emit_range("scalar", s, f"ts{ts_}", f"out{ts_}")
                        for mh in range(2):
                            s.wait_ge(s_f3c, (fe * 2 + mh + 1) * NNS)
                            for tsub in range(NTSF):
                                i = (fe * 2 + mh) * NTSF + tsub
                                if i == 0:
                                    for smx, val in gt.milestone("end"):
                                        s.wait_ge(smx, val)
                                if i >= 2:
                                    s.wait_ge(s_fcm, i - 1)
                                tsg = tf * NTSF + tsub
                                s.activation(CT[i % 2][:], POUT[:, tsub * 512:(tsub + 1) * 512],
                                             AF.Identity,
                                             scale=WALL[:, tsg * NE + e:tsg * NE + e + 1]
                                             ).then_inc(s_fsc, 1)
                for i in range(NTSF):
                    ts_ = (NTF - 1) * NTSF + i
                    p4.emit_range("scalar", s, f"ts{ts_}", f"out{ts_}")

            @blk2.vector
            def _(v):
                entry_waits(v)
                vv2 = [0]
                base_vv = NGH

                def vs(ins):
                    vv2[0] += 1
                    ins.then_inc(s_vv, 1)
                    v.wait_ge(s_vv, base_vv + vv2[0])

                vs(v.memset(COMB[:], 0.0))
                for tsg in range(NTS):
                    v.wait_ge(s_lgm, tsg + 1)
                    ins = v.tensor_copy(LGA[:, tsg * NE:(tsg + 1) * NE], PL)
                    ins.then_inc(s_lge, 1)
                gt.emit_range("vector", v, "tsg0", "end")
                for tf in range(NTF):
                    for e in range(NE):
                        fe = tf * NE + e
                        for ns in range(NNS):
                            u = fe * NNS + ns
                            v.wait_ge(s_fupg, u + 1)
                            v.wait_ge(s_fsil, u + 1)
                            if fe >= 1:
                                v.wait_ge(s_f3c, (2 * fe - 1) * NNS + ns + 1)
                            v.tensor_tensor(HALL[:, ns * TF:(ns + 1) * TF],
                                            PGT[ns % 2][:], SU[ns % 2][:],
                                            ALU.mult).then_inc(s_fhm, 1)
                        if tf >= 1 and e % 2 == 1 and e // 2 < NTSF:
                            ts_ = (tf - 1) * NTSF + e // 2
                            p4.emit_range("vector", v, f"ts{ts_}", f"out{ts_}")
                        for mh in range(2):
                            for tsub in range(NTSF):
                                i = (fe * 2 + mh) * NTSF + tsub
                                v.wait_ge(s_fsc, i + 1)
                                tsg = tf * NTSF + tsub
                                v.tensor_tensor(
                                    COMB[:, tsg * D + mh * 512:tsg * D + (mh + 1) * 512],
                                    COMB[:, tsg * D + mh * 512:tsg * D + (mh + 1) * 512],
                                    CT[i % 2][:], ALU.add).then_inc(s_fcm, 1)
                for i in range(NTSF):
                    ts_ = (NTF - 1) * NTSF + i
                    p4.emit_range("vector", v, f"ts{ts_}", f"out{ts_}")

    return nc


def prep_core_inputs(cfg, core, x, weight, gate_w, rms_w, W1, W2, W3):
    c = derived(cfg)
    T, TQ, DE = c["T"], c["TQ"], c["DE"]
    NNS = c["NNS"]
    b, q = divmod(core, 4)
    t0 = q * TQ
    xb = np.ascontiguousarray(np.roll(x[b], -t0, axis=0)).astype(np.float32, copy=False)
    return {
        "xb": xb,
        "wat": weight,
        "w1t": W1,
        "w2t": W2,
        "w3t": W3,
        "gwt": gate_w,
        "rmswb": rms_w,
    }


def prep_shared(cfg, weight, gate_w, rms_w, W1, W2, W3):
    c = derived(cfg)
    DE, NNS = c["DE"], c["NNS"]
    wat = np.ascontiguousarray(weight, dtype=np.float32)
    w1t = np.ascontiguousarray(
        W1.transpose(0, 2, 1).reshape(NE, D, NNS, 128).transpose(0, 2, 1, 3), dtype=np.float32)
    w2t = np.ascontiguousarray(
        W2.transpose(0, 2, 1).reshape(NE, D, NNS, 128).transpose(0, 2, 1, 3), dtype=np.float32)
    w3t = np.ascontiguousarray(W3.transpose(0, 2, 1), dtype=np.float32)
    gwt = np.ascontiguousarray(gate_w.T, dtype=np.float32)
    rw = np.ascontiguousarray(np.broadcast_to(np.asarray(rms_w, dtype=np.float32)[None, :],
                                              (128, D)))
    return wat, gwt, rw, w1t, w2t, w3t


EYE_NP = np.eye(128, dtype=np.float32)


def numpy_core_reference(cfg, xb, weight, gate_w, rms_w, W1, W2, W3):
    c = derived(cfg)
    T, TQ, DE = c["T"], c["TQ"], c["DE"]
    x = xb.astype(np.float64)

    def ln(a, eps=LN_EPS):
        m = a.mean(-1, keepdims=True)
        v = a.var(-1, keepdims=True)
        return (a - m) / np.sqrt(v + eps)

    h1 = ln(x)
    o = np.zeros((TQ, NH * D))
    for h in range(NH):
        yq = h1[:TQ, h * 64:(h + 1) * 64]
        yk = h1[:, h * 64:(h + 1) * 64]
        s = yq @ yk.T / 8.0
        a = np.exp(s - s.max(-1, keepdims=True))
        a = a / a.sum(-1, keepdims=True)
        o[:, h * D:(h + 1) * D] = a @ h1
    attn = o @ weight.astype(np.float64)
    x2 = x[:TQ] + ln(attn)
    h2 = ln(x2)
    y = h2 / np.sqrt((h2 ** 2).mean(-1, keepdims=True) + RMS_EPS) * rms_w.astype(np.float64)
    logits = y @ gate_w.astype(np.float64).T
    idx = np.argsort(-logits, axis=-1)[:, :2]
    vals = np.take_along_axis(logits, idx, axis=-1)
    w = np.exp(vals - vals.max(-1, keepdims=True))
    w = w / w.sum(-1, keepdims=True)
    wfull = np.zeros_like(logits)
    np.put_along_axis(wfull, idx, w, axis=-1)
    comb = np.zeros((TQ, D))
    for e in range(NE):
        up = y @ W1[e].astype(np.float64).T
        gate = y @ W2[e].astype(np.float64).T
        hh = up / (1 + np.exp(-up)) * gate
        oe = hh @ W3[e].astype(np.float64).T
        comb += oe * wfull[:, e:e + 1]
    ffn = h2 + comb
    return x2 + ln(ffn)


_NC_CACHE = {}


def _get_program():
    key = "full"
    if key not in _NC_CACHE:
        _NC_CACHE[key] = build_program(FULL)
    return _NC_CACHE[key]


def kernel(L0_attn_x, L0_attn_weight, L0_ffn_gate_w, L0_ffn_rms_w,
           L0_ffn_W1, L0_ffn_W2, L0_ffn_W3):
    from concourse.bass_utils import run_bass_kernel_spmd
    cfg = FULL
    c = derived(cfg)
    T, TQ = c["T"], c["TQ"]
    x = np.asarray(L0_attn_x, dtype=np.float32)
    wat, gwt, rw, w1t, w2t, w3t = prep_shared(
        cfg, np.asarray(L0_attn_weight, dtype=np.float32),
        np.asarray(L0_ffn_gate_w, dtype=np.float32),
        np.asarray(L0_ffn_rms_w, dtype=np.float32),
        np.asarray(L0_ffn_W1, dtype=np.float32),
        np.asarray(L0_ffn_W2, dtype=np.float32),
        np.asarray(L0_ffn_W3, dtype=np.float32))
    in_maps = []
    for core in range(8):
        b, q = divmod(core, 4)
        xb = np.ascontiguousarray(np.roll(x[b], -q * TQ, axis=0))
        in_maps.append({"xb": xb, "wat": wat, "w1t": w1t, "w2t": w2t,
                        "w3t": w3t, "gwt": gwt, "rmswb": rw, "eye": EYE_NP})
    nc = _get_program()
    res = run_bass_kernel_spmd(nc, in_maps, core_ids=list(range(8)))
    out = np.empty((2, T, D), dtype=np.float32)
    for core in range(8):
        b, q = divmod(core, 4)
        out[b, q * TQ:(q + 1) * TQ] = np.asarray(res.results[core]["out"])
    return out


# version 07
# speedup vs baseline: 1.0680x; 1.0320x over previous
import numpy as np
import contextlib
import concourse.bass as bass
import concourse.mybir as mybir

F32 = mybir.dt.float32
BF16 = mybir.dt.bfloat16
AF = mybir.ActivationFunctionType
ALU = mybir.AluOpType
AX = mybir.AxisListType

FULL = dict(T=8192, TQ=2048, DE=2816, TF=512)
MINI = dict(T=1024, TQ=256, DE=256, TF=256)

D = 1024
NH = 16
NE = 8
TC = 256
EXP_SCALE = 0.125
EXP_BIAS = -8.0
LN_EPS = 1e-5
RMS_EPS = 1e-6


def derived(cfg):
    c = dict(cfg)
    c["NTC"] = c["TQ"] // TC
    c["NSCH"] = c["T"] // 256
    c["SRES"] = c["T"] // 2
    c["NRES"] = c["SRES"] // 256
    c["NVG"] = (c["T"] - c["SRES"]) // 256
    c["NNS"] = c["DE"] // 128
    c["NTF"] = c["TQ"] // c["TF"]
    c["NTSF"] = c["TF"] // 128
    c["NTS"] = c["TQ"] // 128
    c["NGH"] = c["NTC"] * NH
    c["NST"] = c["T"] // 128
    return c


class Chain:

    def __init__(self, sems):
        self.sems = sems
        self.steps = []
        self.marks = {}

    def step(self, engine, fn, extra=()):
        self.steps.append((engine, fn, list(extra)))

    def mark(self, key):
        self.marks[key] = len(self.steps)

    def _count(self, idx, engine):
        return sum(1 for e, _, _ in self.steps[:idx] if e == engine)

    def milestone(self, key):
        idx = self.marks[key]
        return [(self.sems[e], self._count(idx, e)) for e in self.sems]

    def emit_range(self, engine, eng, lo_mark, hi_mark):
        lo = self.marks[lo_mark]
        hi = self.marks[hi_mark]
        for i in range(lo, hi):
            e, fn, extra = self.steps[i]
            if e != engine:
                continue
            for semx, val in extra:
                if val > 0:
                    eng.wait_ge(semx, val)
            for ename, semx in self.sems.items():
                vcnt = self._count(i, ename)
                if vcnt > 0:
                    eng.wait_ge(semx, vcnt)
            fn(eng).then_inc(self.sems[e], 1)


def build_program(cfg, phase_mode="all", no_vdma=False, no_ktw=False):
    c = derived(cfg)
    T, TQ, DE, TF = c["T"], c["TQ"], c["DE"], c["TF"]
    NTC, NSCH, SRES, NRES, NVG = c["NTC"], c["NSCH"], c["SRES"], c["NRES"], c["NVG"]
    NNS, NTF, NTSF, NTS, NGH, NST = c["NNS"], c["NTF"], c["NTSF"], c["NTS"], c["NGH"], c["NST"]
    NRT = SRES // 128

    nc = bass.Bass("TRN2", target_bir_lowering=False)

    xb = nc.dram_tensor("xb", [T, D], F32, kind="ExternalInput").ap()
    wat = nc.dram_tensor("wat", [NH * D, D], F32, kind="ExternalInput").ap()
    w1t = nc.dram_tensor("w1t", [NE, NNS, D, 128], F32, kind="ExternalInput").ap()
    w2t = nc.dram_tensor("w2t", [NE, NNS, D, 128], F32, kind="ExternalInput").ap()
    w3t = nc.dram_tensor("w3t", [NE, DE, D], F32, kind="ExternalInput").ap()
    gwt = nc.dram_tensor("gwt", [D, NE], F32, kind="ExternalInput").ap()
    rmswb = nc.dram_tensor("rmswb", [128, D], F32, kind="ExternalInput").ap()
    eye = nc.dram_tensor("eye", [128, 128], F32, kind="ExternalInput").ap()
    out = nc.dram_tensor("out", [TQ, D], F32, kind="ExternalOutput").ap()

    h1s = nc.dram_tensor("h1s", [T, D], BF16, kind="Internal").ap()
    h1d = nc.dram_tensor("h1d", [8, T, 128], BF16, kind="Internal").ap()
    h1t = nc.dram_tensor("h1t", [D, T], BF16, kind="Internal").ap()
    watb = nc.dram_tensor("watb", [NH * D, D], BF16, kind="Internal").ap()
    w1b = nc.dram_tensor("w1b", [NE, NNS, D, 128], BF16, kind="Internal").ap()
    w2b = nc.dram_tensor("w2b", [NE, NNS, D, 128], BF16, kind="Internal").ap()
    w3b = nc.dram_tensor("w3b", [NE, DE, D], BF16, kind="Internal").ap()
    gwb = nc.dram_tensor("gwb", [D, NE], BF16, kind="Internal").ap()
    x2s = nc.dram_tensor("x2s", [TQ, D], F32, kind="Internal").ap()
    yd = nc.dram_tensor("yd", [8, TQ, 128], BF16, kind="Internal").ap()

    es = contextlib.ExitStack()
    with es:
        def sem(name):
            return es.enter_context(nc.semaphore(name))

        s_x = [sem("s_x0"), sem("s_x1")]
        s_h1w = sem("s_h1w"); s_hw = [sem("s_hw0"), sem("s_hw1")]
        s_tr = sem("s_tr"); s_h1t = sem("s_h1t")
        s_kt = [sem("s_kt0"), sem("s_kt1")]
        s_w = sem("s_w"); s_v = [sem(f"s_v{i}") for i in range(4)]
        s_xr = [sem("s_xr0"), sem("s_xr1")]
        s_x2w = sem("s_x2w"); s_yw = sem("s_yw"); s_cw = sem("s_cw"); s_cf = sem("s_cf")
        s_yt = sem("s_yt")
        s_w1s = [sem("s_w1a"), sem("s_w1b")]; s_w2s = [sem("s_w2a"), sem("s_w2b")]
        s_f3s = [sem(f"s_f3{i}") for i in range(8)]
        s_x4 = [sem("s_x4a"), sem("s_x4b")]
        s_out = sem("s_out"); s_rw = sem("s_rw"); s_eye = sem("s_eye")
        s_smm = sem("s_smm"); s_exp = sem("s_exp"); s_vd = sem("s_vd")
        s_potev = sem("s_potev"); s_pjmm = sem("s_pjmm"); s_pjev = sem("s_pjev")
        s_dns = sem("s_dns"); s_trd = sem("s_trd"); s_rt = sem("s_rt")
        s_p1s = sem("s_p1s"); s_p1v = sem("s_p1v")
        s_ts = sem("s_ts"); s_tv = sem("s_tv")
        s_gs = sem("s_gs"); s_gv = sem("s_gv")
        s_4s = sem("s_4s"); s_4v = sem("s_4v")
        s_vv = sem("s_vv"); s_one = sem("s_one")
        s_lgm = sem("s_lgm"); s_lge = sem("s_lge")
        s_fupm = sem("s_fupm"); s_fupg = sem("s_fupg"); s_fsil = sem("s_fsil")
        s_fhm = sem("s_fhm"); s_fsc = sem("s_fsc"); s_fcm = sem("s_fcm")
        s_f3c = sem("s_f3c")

        ONES = es.enter_context(nc.sbuf_tensor("ONES", [128, 128], BF16))
        EYE = es.enter_context(nc.sbuf_tensor("EYE", [128, 128], F32))
        MAGIC = es.enter_context(nc.sbuf_tensor("MAGIC", [128, 1], mybir.dt.int32))
        EBIA = es.enter_context(nc.sbuf_tensor("EBIA", [128, 1], F32))
        DUMP = es.enter_context(nc.sbuf_tensor("DUMP", [128, D], BF16))
        RMSWB = es.enter_context(nc.sbuf_tensor("RMSWB", [128, D], F32))

        s_p1s2 = sem("s_p1s2"); s_p1v2 = sem("s_p1v2")
        p1 = Chain({"scalar": s_p1s, "vector": s_p1v})
        p1b = Chain({"scalar": s_p1s2, "vector": s_p1v2})
        tl = Chain({"scalar": s_ts, "vector": s_tv})
        gt = Chain({"scalar": s_gs, "vector": s_gv})
        p4 = Chain({"scalar": s_4s, "vector": s_4v})

        a_es = contextlib.ExitStack()
        with a_es:
            def sb(name, shape, dt):
                return a_es.enter_context(nc.sbuf_tensor(name, shape, dt))
            H1R = sb("H1R", [128, NRT * D], BF16)
            XST = [sb(f"XST{i}", [128, D], F32) for i in range(2)]
            H1ST = [sb(f"H1ST{i}", [128, D], BF16) for i in range(2)]
            TR = sb("TR", [128, T], BF16)
            KTB = [sb(f"KTB{i}", [64, T], BF16) for i in range(2)]
            WBUF = sb("WBUF", [128, 8 * D], BF16)
            VR = sb("VR", [128, 8 * D], BF16)
            EB = sb("EB", [128, 1024], BF16)
            OTS = sb("OTS", [128, 8 * TC], BF16)
            DNS = sb("DNS", [128, TC], F32)
            RECT = [sb(f"RECT{i}", [128, 2], F32) for i in range(2)]
            PJA = [sb(f"PJA{i}", [128, 2 * D], F32) for i in range(2)]
            XO = [sb(f"XO{i}", [128, D], F32) for i in range(2)]
            X2T = sb("X2T", [128, D], F32)
            H2T = sb("H2T", [128, D], F32)
            YTL = sb("YTL", [128, D], BF16)
            stnames = ("S1", "S2", "MEAN", "VEPS", "R0", "Y0", "T1", "T3", "RSTD", "BIAS")
            STT = {k: sb("st" + k, [128, 1], F32) for k in stnames}
            STB = {k: sb("sb" + k, [128, 1], F32) for k in stnames}
            PS = a_es.enter_context(nc.psum_tensor("PS", [128, 1024], F32))
            POT = a_es.enter_context(nc.psum_tensor("POT", [128, 8 * TC], F32))
            PDN = a_es.enter_context(nc.psum_tensor("PDN", [128, TC], F32))
            PPJ = a_es.enter_context(nc.psum_tensor("PPJ", [128, 512], F32))

            def h1dst(st):
                return H1R[:, st * D:(st + 1) * D] if st < NRT else H1ST[st % 2][:]

            def v_tile(gh, k, st2):
                stile = 2 * k + st2
                if no_vdma:
                    stile = stile % NRT
                if stile < NRT:
                    return H1R[:, stile * D:(stile + 1) * D]
                off = stile - NRT
                G = gh * NVG + off // 2
                slot = G % 4
                return VR[:, (slot * 2 + off % 2) * D:(slot * 2 + off % 2 + 1) * D]

            def ln_stats(ch, ST, src_tile, eps, with_mean, extra_first=(), tag=None,
                         sq_vector=False, dump=None):
                S1, S2, MEAN, VEPS, R0, Y0, T1, T3, RSTD, BIAS = (ST[k] for k in stnames)
                if tag:
                    ch.mark(tag + ":0")
                if sq_vector:
                    ch.step("vector", lambda v, src=src_tile: v.tensor_mul(
                        DUMP[:], src, src), extra_first)
                    ch.step("vector", lambda v: v.reduce_sum(S2[:], DUMP[:], axis=AX.X))
                else:
                    dmp = DUMP[:] if dump is None else dump
                    ch.step("scalar", lambda s, src=src_tile, dmp=dmp: s.activation(
                        dmp, src, AF.Square, accum_out=S2[:]), extra_first)
                if tag:
                    ch.mark(tag + ":1")
                if with_mean:
                    ch.step("vector", lambda v, src=src_tile: v.reduce_sum(S1[:], src, axis=AX.X))
                    ch.step("vector", lambda v: v.tensor_scalar(MEAN[:], S1[:], 1.0 / D, None, ALU.mult))
                    ch.step("vector", lambda v: v.tensor_mul(T1[:], MEAN[:], MEAN[:]))
                    ch.step("vector", lambda v: v.tensor_scalar(VEPS[:], S2[:], 1.0 / D, None, ALU.mult))
                    ch.step("vector", lambda v: v.tensor_sub(VEPS[:], VEPS[:], T1[:]))
                    ch.step("vector", lambda v, e=eps: v.tensor_scalar(VEPS[:], VEPS[:], e, None, ALU.add))
                else:
                    ch.step("vector", lambda v, e=eps: v.tensor_scalar(
                        VEPS[:], S2[:], 1.0 / D, e, ALU.mult, ALU.add))
                if sq_vector:
                    ch.step("vector", lambda v: v.tensor_scalar(
                        Y0[:].bitcast(mybir.dt.int32), VEPS[:].bitcast(mybir.dt.int32),
                        1, None, ALU.arith_shift_right))
                    ch.step("vector", lambda v: v.tensor_tensor(
                        Y0[:].bitcast(mybir.dt.int32), MAGIC[:],
                        Y0[:].bitcast(mybir.dt.int32), ALU.subtract))
                    for it in range(3):
                        dst = RSTD if it == 2 else Y0
                        ch.step("vector", lambda v: v.tensor_mul(T1[:], Y0[:], Y0[:]))
                        ch.step("vector", lambda v: v.tensor_mul(T1[:], T1[:], VEPS[:]))
                        ch.step("vector", lambda v: v.tensor_scalar(
                            T3[:], T1[:], -0.5, 1.5, ALU.mult, ALU.add))
                        ch.step("vector", lambda v, dst=dst: v.tensor_mul(dst[:], Y0[:], T3[:]))
                else:
                    ch.step("vector", lambda v: v.reciprocal(R0[:], VEPS[:]))
                    if tag:
                        ch.mark(tag + ":2")
                    ch.step("scalar", lambda s: s.activation(Y0[:], R0[:], AF.Sqrt))
                    if tag:
                        ch.mark(tag + ":3")
                    ch.step("vector", lambda v: v.tensor_mul(T1[:], Y0[:], Y0[:]))
                    ch.step("vector", lambda v: v.tensor_mul(T1[:], T1[:], VEPS[:]))
                    ch.step("vector", lambda v: v.tensor_scalar(T3[:], T1[:], -0.5, 1.5, ALU.mult, ALU.add))
                    ch.step("vector", lambda v: v.tensor_mul(RSTD[:], Y0[:], T3[:]))
                if with_mean:
                    ch.step("vector", lambda v: v.scalar_tensor_tensor(
                        BIAS[:], MEAN[:], -1.0, RSTD[:], ALU.mult, ALU.mult))

            def p1_chain(st):
                return p1 if st % 2 == 0 else p1b

            p1.mark("start"); p1b.mark("start")
            for st in range(NST):
                ch = p1_chain(st)
                ST = STT if st % 2 == 0 else STB
                ch.mark(f"t{st}")
                ln_stats(ch, ST, XST[st % 2][:], LN_EPS, True,
                         [(s_x[st % 2], 16 * (st // 2 + 1))], tag=f"p{st}",
                         dump=POT[:, (st % 2) * 1024:(st % 2) * 1024 + 1024])
                h1_extra = ([(s_hw[st % 2], 32 * ((st - NRT) // 2))]
                            if st >= NRT + 2 else [])
                ch.step("scalar", lambda s, st=st, ST=ST: s.activation(
                    h1dst(st), XST[st % 2][:], AF.Identity,
                    bias=ST["BIAS"][:], scale=ST["RSTD"][:]), h1_extra)
                ch.mark(f"h1_{st}")
            p1.mark("end"); p1b.mark("end")

            def p1_milestone(st):
                return p1_chain(st).milestone(f"h1_{st}")

            for tc in range(NTC):
                for tsub in range(2):
                    ts = tc * 2 + tsub
                    tl.mark(f"ts{ts}")
                    PA = PJA[tc % 2][:, tsub * D:(tsub + 1) * D]
                    ln_stats(tl, STT, PA, LN_EPS, True, [(s_pjev, 4 * NH * (tc + 1))],
                             sq_vector=True)
                    x2_extra = [(s_xr[ts % 2], 16 * (ts // 2 + 1))]
                    if ts >= 1:
                        x2_extra.append((s_x2w, 16 * ts))
                    tl.step("vector", lambda v, ts=ts, PA=PA: v.scalar_tensor_tensor(
                        X2T[:], PA, STT["RSTD"][:], XO[ts % 2][:], ALU.mult, ALU.add), x2_extra)
                    tl.mark(f"pa_done{ts}")
                    tl.step("vector", lambda v: v.tensor_scalar(
                        X2T[:], X2T[:], STT["BIAS"][:], None, ALU.add))
                    tl.mark(f"x2_{ts}")
                    ln_stats(tl, STT, X2T[:], LN_EPS, True, sq_vector=True)
                    tl.step("vector", lambda v: v.tensor_scalar(
                        H2T[:], X2T[:], STT["RSTD"][:], STT["BIAS"][:], ALU.mult, ALU.add))
                    tl.mark(f"x2free{ts}")
                    ln_stats(tl, STT, H2T[:], RMS_EPS, False, sq_vector=True)
                    y_extra = [(s_rw, 16)]
                    if ts >= 1:
                        y_extra.append((s_yw, 16 * ts))
                    tl.step("vector", lambda v: v.scalar_tensor_tensor(
                        YTL[:], H2T[:], STT["RSTD"][:], RMSWB[:], ALU.mult, ALU.mult), y_extra)
                    tl.mark(f"y_{ts}")
            tl.mark("end")

            blk = a_es.enter_context(nc.Block())

            @blk.sync
            def _(sy):
                sy.dma_start(out=RMSWB[:], in_=rmswb).then_inc(s_rw, 16)
                sy.dma_start(out=EYE[:], in_=eye).then_inc(s_eye, 16)
                for st in range(NST):
                    if st >= 2:
                        for smx, val in p1_milestone(st - 2):
                            sy.wait_ge(smx, val)
                    sy.dma_start(out=XST[st % 2][:],
                                 in_=xb[st * 128:(st + 1) * 128, :]).then_inc(s_x[st % 2], 16)
                sy.wait_ge(s_h1w, 32 * NRT)
                nhw0 = sum(1 for st in range(NRT, NST) if st % 2 == 0)
                nhw1 = (NST - NRT) - nhw0
                if nhw0:
                    sy.wait_ge(s_hw[0], 32 * nhw0)
                if nhw1:
                    sy.wait_ge(s_hw[1], 32 * nhw1)
                for dt in range(8):
                    if dt >= 1:
                        sy.wait_ge(s_h1t, 16 * dt)
                    sy.dma_start_transpose(out=TR[:], in_=h1d[dt]).then_inc(s_tr, 16)
                    sy.wait_ge(s_tr, 16 * (dt + 1))
                    sy.dma_start(out=h1t[dt * 128:(dt + 1) * 128, :], in_=TR[:]).then_inc(s_h1t, 16)
                def kt_load(sy, gh):
                    if gh >= NGH:
                        return
                    hh_ = gh % NH
                    if gh >= 2:
                        sy.wait_ge(s_smm, (gh - 1) * NSCH)
                    sy.wait_ge(s_h1t, 16 * (hh_ // 2 + 1))
                    sy.dma_start(out=KTB[gh % 2][0:64, :],
                                 in_=h1t[hh_ * 64:(hh_ + 1) * 64, :]).then_inc(s_kt[gh % 2], 16)

                kt_load(sy, 0)
                for tc in range(NTC):
                    for h in range(NH):
                        gh = tc * NH + h
                        kt_load(sy, gh + 1)
                        for g in range(0 if no_vdma else NVG):
                            G = gh * NVG + g
                            if G >= 4:
                                ghp, gp = divmod(G - 4, NVG)
                                sy.wait_ge(s_vd, ghp * NSCH + NRES + gp + 1)
                            r0 = SRES + g * 256
                            sy.dma_start(
                                out=VR[:, ((G % 4) * 2) * D:((G % 4) * 2 + 2) * D]
                                .rearrange("p (t d) -> p t d", d=D),
                                in_=h1s[r0:r0 + 256, :].rearrange("(t p) d -> p t d", p=128)
                            ).then_inc(s_v[G % 4], 16)
                        if gh >= 1:
                            sy.wait_ge(s_pjmm, 4 * gh)
                        if tc == 0:
                            sy.wait_ge(s_cw, 16 * (h + 1))
                        sy.dma_start(out=WBUF[:].rearrange("p (c d) -> p c d", d=D),
                                     in_=watb[h * D:(h + 1) * D, :]
                                     .rearrange("(c p) d -> p c d", p=128)).then_inc(s_w, 16)

            @blk.gpsimd
            def _(g):
                for st in range(NST):
                    for smx, val in p1_milestone(st):
                        g.wait_ge(smx, val)
                    src = h1dst(st)
                    dsem = s_h1w if st < NRT else s_hw[st % 2]
                    g.dma_start(out=h1s[st * 128:(st + 1) * 128, :], in_=src).then_inc(dsem, 16)
                    g.dma_start(out=h1d[:, st * 128:(st + 1) * 128, :].rearrange("dt p d -> p dt d"),
                                in_=src.rearrange("p (dt d) -> p dt d", d=128)).then_inc(dsem, 16)
                for ci in range(NH):
                    if ci >= 1:
                        g.wait_ge(s_cw, 16 * ci)
                    g.dma_start(out=watb[ci * D:(ci + 1) * D, :],
                                in_=wat[ci * D:(ci + 1) * D, :]).then_inc(s_cw, 16)
                g.dma_start(out=gwb, in_=gwt).then_inc(s_cf, 16)
                casts = []
                for e in range(NE):
                    casts.append(lambda g, e=e: g.dma_start(out=w1b[e], in_=w1t[e]).then_inc(s_cf, 16))
                    casts.append(lambda g, e=e: g.dma_start(out=w2b[e], in_=w2t[e]).then_inc(s_cf, 16))
                    casts.append(lambda g, e=e: g.dma_start(out=w3b[e], in_=w3t[e]).then_inc(s_cf, 16))
                nts2 = 2 * NTC
                per = (len(casts) + nts2 - 1) // nts2

                def xo_load(g, ts):
                    if ts >= nts2:
                        return
                    if ts >= 2:
                        for smx, val in tl.milestone(f"x2_{ts - 2}"):
                            g.wait_ge(smx, val)
                    g.dma_start(out=XO[ts % 2][:],
                                in_=xb[ts * 128:(ts + 1) * 128, :]).then_inc(s_xr[ts % 2], 16)

                xo_load(g, 0)
                xo_load(g, 1)
                for ts in range(nts2):
                    for fn in casts[ts * per:(ts + 1) * per]:
                        fn(g)
                    xo_load(g, ts + 2)
                    for smx, val in tl.milestone(f"x2free{ts}"):
                        g.wait_ge(smx, val)
                    g.dma_start(out=x2s[ts * 128:(ts + 1) * 128, :], in_=X2T[:]).then_inc(s_x2w, 16)
                    for smx, val in tl.milestone(f"y_{ts}"):
                        g.wait_ge(smx, val)
                    g.dma_start(out=yd[:, ts * 128:(ts + 1) * 128, :].rearrange("dt p d -> p dt d"),
                                in_=YTL[:].rearrange("p (dt d) -> p dt d", d=128)).then_inc(s_yw, 16)

            @blk.tensor
            def _(t):
                t.wait_ge(s_one, 2)
                t.wait_ge(s_eye, 16)

                def emit_S(gh, tc, k):
                    sg = gh * NSCH + k
                    if k == 0:
                        t.wait_ge(s_kt[gh % 2], 16 * (gh // 2 + 1))
                    if sg >= 2:
                        t.wait_ge(s_exp, sg - 1)
                    KT = KTB[gh % 2]
                    base = (k % 2) * 512
                    t.matmul(PS[:, base:base + TC],
                             KT[0:64, (2 * k) * 128:(2 * k + 1) * 128],
                             KT[0:64, tc * TC:(tc + 1) * TC], start=True, stop=True)
                    t.matmul(PS[:, base + TC:base + 512],
                             KT[0:64, (2 * k + 1) * 128:(2 * k + 2) * 128],
                             KT[0:64, tc * TC:(tc + 1) * TC], start=True, stop=True).then_inc(s_smm, 1)

                def emit_V(gh, k):
                    sg = gh * NSCH + k
                    t.wait_ge(s_exp, sg + 1)
                    if k == 0 and gh >= 1:
                        t.wait_ge(s_potev, 2 * gh)
                        t.wait_ge(s_dns, gh)
                    if k >= NRES and not no_vdma:
                        G = gh * NVG + (k - NRES)
                        t.wait_ge(s_v[G % 4], 16 * (G // 4 + 1))
                    last_k = (k == NSCH - 1)
                    ins = None
                    for st2 in range(2):
                        rhs = EB[:, ((k % 2) * 2 + st2) * TC:((k % 2) * 2 + st2 + 1) * TC]
                        first = (k == 0 and st2 == 0)
                        last = (last_k and st2 == 1)
                        t.matmul(PDN[:], ONES[:], rhs, start=first, stop=last)
                        vt = v_tile(gh, k, st2)
                        for cc in range(8):
                            ins = t.matmul(POT[:, cc * TC:(cc + 1) * TC],
                                           vt[:, cc * 128:(cc + 1) * 128], rhs,
                                           start=(first and cc % 2 == 0),
                                           stop=(last and cc % 2 == 1))
                    ins.then_inc(s_vd, 1)

                def emit_pq(pgh, q):
                    tsub, dh = divmod(q, 2)
                    t.wait_ge(s_potev, 2 * (pgh + 1))
                    t.wait_ge(s_w, 16 * (pgh + 1))
                    t.wait_ge(s_pjev, pgh * 4 + q)
                    if q == 0:
                        t.wait_ge(s_rt, pgh + 1)
                    ins = None
                    for cc in range(8):
                        ins = t.matmul(PPJ[:],
                                       OTS[:, cc * TC + tsub * 128:cc * TC + (tsub + 1) * 128],
                                       WBUF[:, cc * D + dh * 512:cc * D + (dh + 1) * 512],
                                       start=(cc == 0), stop=(cc == 7))
                    ins.then_inc(s_pjmm, 1)

                pos2 = {}
                for q in range(4):
                    kq = max(1, min(NSCH - 1, (q + 1) * NSCH // 5))
                    pos2.setdefault(kq, []).append(q)

                def emit_trd(pgh):
                    t.wait_ge(s_dns, pgh + 1)
                    t.wait_ge(s_pjev, 4 * pgh)
                    t.transpose(PPJ[:, 0:128], DNS[:, 0:128], EYE[:])
                    t.transpose(PPJ[:, 128:256], DNS[:, 128:256], EYE[:]).then_inc(s_trd, 1)

                for tc in range(NTC):
                    for h in range(NH):
                        gh = tc * NH + h
                        emit_S(gh, tc, 0)
                        for k in range(1, NSCH):
                            emit_S(gh, tc, k)
                            if k == 1 and gh >= 1:
                                emit_trd(gh - 1)
                            if gh >= 1:
                                for q in pos2.get(k, ()):
                                    emit_pq(gh - 1, q)
                            emit_V(gh, k - 1)
                        emit_V(gh, NSCH - 1)
                emit_trd(NGH - 1)
                for q in range(4):
                    emit_pq(NGH - 1, q)

            @blk.scalar
            def _(s):
                s.wait_ge(s_one, 2)
                for st0 in range(0, NST, 2):
                    for seg in ((f"p{{}}:0", f"p{{}}:1"), (f"p{{}}:2", f"p{{}}:3"),
                                (f"p{{}}:3", "h1_{}")):
                        for st in (st0, st0 + 1):
                            lo = seg[0].format(st)
                            hi = seg[1].format(st) if "h1" not in seg[1] else f"h1_{st}"
                            p1_chain(st).emit_range("scalar", s, lo, hi)
                for tc in range(NTC):
                    for h in range(NH):
                        gh = tc * NH + h
                        for k in range(NSCH):
                            sg = gh * NSCH + k
                            s.wait_ge(s_smm, sg + 1)
                            if sg >= 2:
                                s.wait_ge(s_vd, sg - 1)
                            s.activation(EB[:, (k % 2) * 512:(k % 2) * 512 + 512],
                                         PS[:, (k % 2) * 512:(k % 2) * 512 + 512],
                                         AF.Exp, scale=EXP_SCALE, bias=EBIA[:]).then_inc(s_exp, 1)
                        s.wait_ge(s_vd, (gh + 1) * NSCH)
                        if gh >= 1:
                            s.wait_ge(s_pjmm, 4 * gh)
                        s.activation(OTS[:, 0:4 * TC], POT[:, 0:4 * TC],
                                     AF.Identity).then_inc(s_potev, 1)
                        if h == 1 and tc >= 1:
                            tl.emit_range("scalar", s, f"ts{(tc - 1) * 2}", f"y_{(tc - 1) * 2 + 1}")
                tl.emit_range("scalar", s, f"ts{(NTC - 1) * 2}", "end")

            @blk.vector
            def _(v):
                vv = [0]

                def vsync(ins):
                    vv[0] += 1
                    ins.then_inc(s_vv, 1)
                    v.wait_ge(s_vv, vv[0])

                v.memset(ONES[:], 1.0).then_inc(s_one, 1)
                v.memset(EBIA[:], EXP_BIAS).then_inc(s_one, 1)
                v.memset(MAGIC[:], 0x5f3759df)
                for st0 in range(0, NST, 2):
                    for seg in (("p{}:0", "p{}:2"), ("p{}:3", "h1_{}")):
                        for st in (st0, st0 + 1):
                            lo = seg[0].format(st)
                            hi = seg[1].format(st)
                            p1_chain(st).emit_range("vector", v, lo, hi)

                def emit_dns(gh):
                    v.wait_ge(s_vd, (gh + 1) * NSCH)
                    v.tensor_copy(DNS[:], PDN[:]).then_inc(s_dns, 1)
                    if gh >= 1:
                        v.wait_ge(s_pjmm, 4 * gh)
                    v.tensor_copy(OTS[:, 4 * TC:8 * TC],
                                  POT[:, 4 * TC:8 * TC]).then_inc(s_potev, 1)

                def emit_recip2(pgh):
                    v.wait_ge(s_trd, pgh + 1)
                    vsync(v.reciprocal(RECT[pgh % 2][:, 0:1], PPJ[:, 0:1]))
                    ins = v.reciprocal(RECT[pgh % 2][:, 1:2], PPJ[:, 128:129])
                    ins.then_inc(s_rt, 1)
                    vv[0] += 0

                def emit_adds(pgh):
                    ptc = pgh // NH
                    for q in range(4):
                        tsub, dh = divmod(q, 2)
                        dst = PJA[ptc % 2][:, tsub * D + dh * 512:tsub * D + (dh + 1) * 512]
                        v.wait_ge(s_pjmm, pgh * 4 + q + 1)
                        rsc = RECT[pgh % 2][:, tsub:tsub + 1]
                        if pgh % NH == 0:
                            if ptc >= 2:
                                for smx, val in tl.milestone(f"pa_done{(ptc - 2) * 2 + 1}"):
                                    v.wait_ge(smx, val)
                            ins = v.tensor_scalar(dst, PPJ[:], rsc, None, ALU.mult)
                        else:
                            ins = v.scalar_tensor_tensor(dst, PPJ[:], rsc, dst, ALU.mult, ALU.add)
                        ins.then_inc(s_pjev, 1)

                for tc in range(NTC):
                    for h in range(NH):
                        gh = tc * NH + h
                        if gh >= 1:
                            emit_recip2(gh - 1)
                            emit_adds(gh - 1)
                        emit_dns(gh)
                        if h == 1 and tc >= 1:
                            tl.emit_range("vector", v, f"ts{(tc - 1) * 2}", f"y_{(tc - 1) * 2 + 1}")
                emit_recip2(NGH - 1)
                emit_adds(NGH - 1)
                tl.emit_range("vector", v, f"ts{(NTC - 1) * 2}", "end")

        if phase_mode == "attn":
            with nc.Block() as blk2s:
                @blk2s.sync
                def _(sy):
                    eng = sy
                    eng.wait_ge(s_yw, 16 * 2 * NTC)
                    eng.wait_ge(s_x2w, 16 * 2 * NTC)
                    for smx, val in tl.milestone("end"):
                        eng.wait_ge(smx, val)
                    for ts in range(NTS):
                        sy.dma_start(out=out[ts * 128:(ts + 1) * 128, :],
                                     in_=x2s[ts * 128:(ts + 1) * 128, :]).then_inc(s_out, 16)
                    sy.wait_ge(s_out, 16 * NTS)
            return nc

        f_es = contextlib.ExitStack()
        with f_es:
            def fb(name, shape, dt):
                return f_es.enter_context(nc.sbuf_tensor(name, shape, dt))
            YT = fb("YT", [128, 8 * TQ], BF16)
            HALL = fb("HALL", [128, NNS * TF], BF16)
            COMB = fb("COMB", [128, NTS * D], F32)
            W1R = [fb(f"W1R{i}", [128, 8 * 128], BF16) for i in range(2)]
            W2R = [fb(f"W2R{i}", [128, 8 * 128], BF16) for i in range(2)]
            W3R = fb("W3R", [128, 8 * D], BF16)
            SU = [fb(f"SU{i}", [128, TF], BF16) for i in range(2)]
            CT = [fb(f"CT{i}", [128, 512], F32) for i in range(2)]
            GWB = fb("GWB", [128, 8 * NE], BF16)
            LGA = fb("LGA", [128, NTS * NE], F32)
            WALL = fb("WALL", [128, NTS * NE], F32)
            X2R = [fb(f"X2R{i}", [128, D], F32) for i in range(2)]
            H2P = fb("H2P", [128, D], F32)
            FFT = fb("FFT", [128, D], F32)
            FLN = fb("FLN", [128, D], F32)
            OUTT = fb("OUTT", [128, D], F32)
            G_ = {k: fb("g" + k, [128, NE], F32) for k in ("MK1", "LM", "MK2", "EL", "TMP")}
            G1 = {k: fb("g1" + k, [128, 1], F32)
                  for k in ("M1", "M2", "NM1", "D21", "E2", "DN2", "RCW")}
            ST2 = {k: fb("s2" + k, [128, 1], F32) for k in stnames}
            PUP = [f_es.enter_context(nc.psum_tensor(f"PUP{i}", [128, TF], F32)) for i in range(2)]
            PGT = [f_es.enter_context(nc.psum_tensor(f"PGT{i}", [128, TF], F32)) for i in range(2)]
            POUT = f_es.enter_context(nc.psum_tensor("POUT", [128, NTSF * 512], F32))
            PL = POUT[:, 0:NE]

            NFC = 1 + 3 * NE

            def entry_waits(eng):
                eng.wait_ge(s_vd, NGH * NSCH)
                eng.wait_ge(s_exp, NGH * NSCH)
                eng.wait_ge(s_pjmm, 4 * NGH)
                eng.wait_ge(s_pjev, 4 * NGH)
                eng.wait_ge(s_potev, 2 * NGH)
                eng.wait_ge(s_dns, NGH)
                eng.wait_ge(s_trd, NGH)
                eng.wait_ge(s_rt, NGH)
                eng.wait_ge(s_yw, 16 * 2 * NTC)
                eng.wait_ge(s_x2w, 16 * 2 * NTC)
                eng.wait_ge(s_kt[0], 16 * ((NGH + 1) // 2))
                eng.wait_ge(s_kt[1], 16 * (NGH // 2))
                eng.wait_ge(s_w, 16 * NGH)
                ngv = 0 if no_vdma else NGH * NVG
                for si in range(4):
                    cnt = (ngv - si + 3) // 4
                    if cnt > 0:
                        eng.wait_ge(s_v[si], 16 * cnt)
                for smx, val in tl.milestone("end"):
                    eng.wait_ge(smx, val)

            for tsg in range(NTS):
                gt.mark(f"tsg{tsg}")
                LG = LGA[:, tsg * NE:(tsg + 1) * NE]
                WF = WALL[:, tsg * NE:(tsg + 1) * NE]
                gt.step("vector", lambda v, LG=LG: v.reduce_max(G1["M1"][:], LG, axis=AX.X),
                        [(s_lge, tsg + 1)])
                gt.step("vector", lambda v, LG=LG: v.tensor_scalar(
                    G_["MK1"][:], LG, G1["M1"][:], None, ALU.is_ge))
                gt.step("vector", lambda v, LG=LG: v.scalar_tensor_tensor(
                    G_["LM"][:], G_["MK1"][:], -1e30, LG, ALU.mult, ALU.add))
                gt.step("vector", lambda v: v.reduce_max(G1["M2"][:], G_["LM"][:], axis=AX.X))
                gt.step("vector", lambda v, LG=LG: v.tensor_scalar(
                    G_["MK2"][:], LG, G1["M2"][:], None, ALU.is_ge))
                gt.step("vector", lambda v: v.tensor_scalar_mul(G1["NM1"][:], G1["M1"][:], -1.0))
                gt.step("scalar", lambda s, LG=LG: s.activation(
                    G_["EL"][:], LG, AF.Exp, bias=G1["NM1"][:]))
                gt.step("vector", lambda v: v.tensor_sub(G1["D21"][:], G1["M2"][:], G1["M1"][:]))
                gt.step("scalar", lambda s: s.activation(G1["E2"][:], G1["D21"][:], AF.Exp))
                gt.step("vector", lambda v: v.tensor_scalar(
                    G1["DN2"][:], G1["E2"][:], 1.0, None, ALU.add))
                gt.step("vector", lambda v: v.reciprocal(G1["RCW"][:], G1["DN2"][:]))
                gt.step("vector", lambda v: v.tensor_scalar(
                    G_["TMP"][:], G_["EL"][:], G1["RCW"][:], None, ALU.mult))
                gt.step("vector", lambda v, WF=WF: v.tensor_mul(WF, G_["TMP"][:], G_["MK2"][:]))
            gt.mark("end")

            for ts in range(NTS):
                tf = ts // NTSF
                p4.mark(f"ts{ts}")
                ln_extra = [(s_x4[ts % 2], 16 * (ts // 2 + 1))]
                ln_stats(p4, ST2, X2R[ts % 2][:], LN_EPS, True, ln_extra, sq_vector=True)
                p4.step("vector", lambda v, ts=ts: v.tensor_scalar(
                    H2P[:], X2R[ts % 2][:], ST2["RSTD"][:], ST2["BIAS"][:], ALU.mult, ALU.add))
                p4.step("vector", lambda v, ts=ts: v.tensor_add(
                    FFT[:], H2P[:], COMB[:, ts * D:(ts + 1) * D]),
                    [(s_fcm, (tf + 1) * 2 * NE * NTSF)])
                ln_stats(p4, ST2, FFT[:], LN_EPS, True, sq_vector=True)
                p4.step("vector", lambda v: v.tensor_scalar(
                    FLN[:], FFT[:], ST2["RSTD"][:], ST2["BIAS"][:], ALU.mult, ALU.add))
                p4.step("vector", lambda v, ts=ts: v.tensor_add(OUTT[:], X2R[ts % 2][:], FLN[:]),
                        [(s_out, 16 * ts)] if ts >= 1 else [])
                p4.mark(f"out{ts}")
            p4.mark("end")

            blk2 = f_es.enter_context(nc.Block())

            @blk2.sync
            def _(sy):
                entry_waits(sy)
                sy.wait_ge(s_cf, 16 * NFC)
                for dt in range(8):
                    sy.dma_start_transpose(out=YT[:, dt * TQ:(dt + 1) * TQ],
                                           in_=yd[dt]).then_inc(s_yt, 16)
                sy.dma_start(out=GWB[:].rearrange("p (dt e) -> p dt e", e=NE),
                             in_=gwb.rearrange("(dt p) e -> p dt e", p=128)).then_inc(s_rw, 16)
                for tf in range(NTF):
                    for e in range(NE):
                        fe = tf * NE + e
                        for ns in range(NNS):
                            u = fe * NNS + ns
                            if u >= 2:
                                sy.wait_ge(s_fupm, u - 1)
                            sy.dma_start(out=W1R[ns % 2][:].rearrange("p (dt n) -> p dt n", n=128),
                                         in_=w1b[e, ns].rearrange("(dt p) n -> p dt n", p=128)
                                         ).then_inc(s_w1s[ns % 2], 16)
                            if u >= 2:
                                sy.wait_ge(s_fupg, u - 1)
                            sy.dma_start(out=W2R[ns % 2][:].rearrange("p (dt n) -> p dt n", n=128),
                                         in_=w2b[e, ns].rearrange("(dt p) n -> p dt n", p=128)
                                         ).then_inc(s_w2s[ns % 2], 16)
                        for mh in range(2):
                            for ns in range(NNS):
                                j = (fe * 2 + mh) * NNS + ns
                                if j >= 8:
                                    sy.wait_ge(s_f3c, j - 7)
                                sy.dma_start(out=W3R[:, (j % 8) * D:(j % 8 + 1) * D],
                                             in_=w3b[e, ns * 128:(ns + 1) * 128, :]
                                             ).then_inc(s_f3s[j % 8], 16)
                sy.wait_ge(s_out, 16 * NTS)

            @blk2.gpsimd
            def _(g):
                entry_waits(g)
                for ts in range(NTS):
                    if ts >= 2:
                        for smx, val in p4.milestone(f"out{ts - 2}"):
                            g.wait_ge(smx, val)
                    g.dma_start(out=X2R[ts % 2][:],
                                in_=x2s[ts * 128:(ts + 1) * 128, :]).then_inc(s_x4[ts % 2], 16)
                    if ts >= 1:
                        for smx, val in p4.milestone(f"out{ts - 1}"):
                            g.wait_ge(smx, val)
                        g.dma_start(out=out[(ts - 1) * 128:ts * 128, :],
                                    in_=OUTT[:]).then_inc(s_out, 16)
                for smx, val in p4.milestone(f"out{NTS - 1}"):
                    g.wait_ge(smx, val)
                g.dma_start(out=out[(NTS - 1) * 128:NTS * 128, :],
                            in_=OUTT[:]).then_inc(s_out, 16)

            @blk2.tensor
            def _(t):
                entry_waits(t)
                t.wait_ge(s_yt, 16 * 8)
                t.wait_ge(s_rw, 32)
                for tsg in range(NTS):
                    if tsg >= 1:
                        t.wait_ge(s_lge, tsg)
                    ins = None
                    for dt in range(8):
                        ins = t.matmul(PL,
                                       YT[:, dt * TQ + tsg * 128:dt * TQ + (tsg + 1) * 128],
                                       GWB[:, dt * NE:(dt + 1) * NE],
                                       start=(dt == 0), stop=(dt == 7))
                    ins.then_inc(s_lgm, 1)
                t.wait_ge(s_lge, NTS)
                for tf in range(NTF):
                    for e in range(NE):
                        fe = tf * NE + e
                        for ns in range(NNS):
                            u = fe * NNS + ns
                            t.wait_ge(s_w1s[ns % 2], 16 * (u // 2 + 1))
                            if u >= 2:
                                t.wait_ge(s_fsil, u - 1)
                            ins = None
                            for dt in range(8):
                                ins = t.matmul(PUP[ns % 2][:],
                                               W1R[ns % 2][:, dt * 128:(dt + 1) * 128],
                                               YT[:, dt * TQ + tf * TF:dt * TQ + (tf + 1) * TF],
                                               start=(dt == 0), stop=(dt == 7))
                            ins.then_inc(s_fupm, 1)
                            if u >= 2:
                                t.wait_ge(s_fhm, u - 1)
                            t.wait_ge(s_w2s[ns % 2], 16 * (u // 2 + 1))
                            ins = None
                            for dt in range(8):
                                ins = t.matmul(PGT[ns % 2][:],
                                               W2R[ns % 2][:, dt * 128:(dt + 1) * 128],
                                               YT[:, dt * TQ + tf * TF:dt * TQ + (tf + 1) * TF],
                                               start=(dt == 0), stop=(dt == 7))
                            ins.then_inc(s_fupg, 1)
                        for mh in range(2):
                            if fe * 2 + mh >= 1:
                                t.wait_ge(s_fsc, (fe * 2 + mh) * NTSF)
                            for ns in range(NNS):
                                j = (fe * 2 + mh) * NNS + ns
                                t.wait_ge(s_f3s[j % 8], 16 * (j // 8 + 1))
                                if mh == 0:
                                    t.wait_ge(s_fhm, fe * NNS + ns + 1)
                                ins = None
                                for tsub in range(NTSF):
                                    ins = t.matmul(
                                        POUT[:, tsub * 512:(tsub + 1) * 512],
                                        HALL[:, ns * TF + tsub * 128:ns * TF + (tsub + 1) * 128],
                                        W3R[:, (j % 8) * D + mh * 512:(j % 8) * D + (mh + 1) * 512],
                                        start=(ns == 0), stop=(ns == NNS - 1))
                                ins.then_inc(s_f3c, 1)

            @blk2.scalar
            def _(s):
                entry_waits(s)
                gt.emit_range("scalar", s, "tsg0", "end")

                def p4_seg(i, tfx):
                    ts = tfx * NTSF + i
                    lo = f"ts{ts}"
                    hi = f"out{ts}" if i < NTSF - 1 else f"out{ts}"
                    return lo, f"out{ts}"

                for tf in range(NTF):
                    for e in range(NE):
                        fe = tf * NE + e
                        for ns in range(NNS):
                            u = fe * NNS + ns
                            s.wait_ge(s_fupm, u + 1)
                            if u >= 2:
                                s.wait_ge(s_fhm, u - 1)
                            s.activation(SU[ns % 2][:], PUP[ns % 2][:], AF.Silu).then_inc(s_fsil, 1)
                        if tf >= 1 and e % 2 == 1 and e // 2 < NTSF:
                            ts_ = (tf - 1) * NTSF + e // 2
                            p4.emit_range("scalar", s, f"ts{ts_}", f"out{ts_}")
                        for mh in range(2):
                            s.wait_ge(s_f3c, (fe * 2 + mh + 1) * NNS)
                            for tsub in range(NTSF):
                                i = (fe * 2 + mh) * NTSF + tsub
                                if i == 0:
                                    for smx, val in gt.milestone("end"):
                                        s.wait_ge(smx, val)
                                if i >= 2:
                                    s.wait_ge(s_fcm, i - 1)
                                tsg = tf * NTSF + tsub
                                s.activation(CT[i % 2][:], POUT[:, tsub * 512:(tsub + 1) * 512],
                                             AF.Identity,
                                             scale=WALL[:, tsg * NE + e:tsg * NE + e + 1]
                                             ).then_inc(s_fsc, 1)
                for i in range(NTSF):
                    ts_ = (NTF - 1) * NTSF + i
                    p4.emit_range("scalar", s, f"ts{ts_}", f"out{ts_}")

            @blk2.vector
            def _(v):
                entry_waits(v)
                vv2 = [0]
                base_vv = NGH

                def vs(ins):
                    vv2[0] += 1
                    ins.then_inc(s_vv, 1)
                    v.wait_ge(s_vv, base_vv + vv2[0])

                vs(v.memset(COMB[:], 0.0))
                for tsg in range(NTS):
                    v.wait_ge(s_lgm, tsg + 1)
                    ins = v.tensor_copy(LGA[:, tsg * NE:(tsg + 1) * NE], PL)
                    ins.then_inc(s_lge, 1)
                gt.emit_range("vector", v, "tsg0", "end")
                for tf in range(NTF):
                    for e in range(NE):
                        fe = tf * NE + e
                        for ns in range(NNS):
                            u = fe * NNS + ns
                            v.wait_ge(s_fupg, u + 1)
                            v.wait_ge(s_fsil, u + 1)
                            if fe >= 1:
                                v.wait_ge(s_f3c, (2 * fe - 1) * NNS + ns + 1)
                            v.tensor_tensor(HALL[:, ns * TF:(ns + 1) * TF],
                                            PGT[ns % 2][:], SU[ns % 2][:],
                                            ALU.mult).then_inc(s_fhm, 1)
                        if tf >= 1 and e % 2 == 1 and e // 2 < NTSF:
                            ts_ = (tf - 1) * NTSF + e // 2
                            p4.emit_range("vector", v, f"ts{ts_}", f"out{ts_}")
                        for mh in range(2):
                            for tsub in range(NTSF):
                                i = (fe * 2 + mh) * NTSF + tsub
                                v.wait_ge(s_fsc, i + 1)
                                tsg = tf * NTSF + tsub
                                v.tensor_tensor(
                                    COMB[:, tsg * D + mh * 512:tsg * D + (mh + 1) * 512],
                                    COMB[:, tsg * D + mh * 512:tsg * D + (mh + 1) * 512],
                                    CT[i % 2][:], ALU.add).then_inc(s_fcm, 1)
                for i in range(NTSF):
                    ts_ = (NTF - 1) * NTSF + i
                    p4.emit_range("vector", v, f"ts{ts_}", f"out{ts_}")

    return nc


def prep_core_inputs(cfg, core, x, weight, gate_w, rms_w, W1, W2, W3):
    c = derived(cfg)
    T, TQ, DE = c["T"], c["TQ"], c["DE"]
    NNS = c["NNS"]
    b, q = divmod(core, 4)
    t0 = q * TQ
    xb = np.ascontiguousarray(np.roll(x[b], -t0, axis=0)).astype(np.float32, copy=False)
    return {
        "xb": xb,
        "wat": weight,
        "w1t": W1,
        "w2t": W2,
        "w3t": W3,
        "gwt": gate_w,
        "rmswb": rms_w,
    }


def prep_shared(cfg, weight, gate_w, rms_w, W1, W2, W3):
    c = derived(cfg)
    DE, NNS = c["DE"], c["NNS"]
    wat = np.ascontiguousarray(weight, dtype=np.float32)
    w1t = np.ascontiguousarray(
        W1.transpose(0, 2, 1).reshape(NE, D, NNS, 128).transpose(0, 2, 1, 3), dtype=np.float32)
    w2t = np.ascontiguousarray(
        W2.transpose(0, 2, 1).reshape(NE, D, NNS, 128).transpose(0, 2, 1, 3), dtype=np.float32)
    w3t = np.ascontiguousarray(W3.transpose(0, 2, 1), dtype=np.float32)
    gwt = np.ascontiguousarray(gate_w.T, dtype=np.float32)
    rw = np.ascontiguousarray(np.broadcast_to(np.asarray(rms_w, dtype=np.float32)[None, :],
                                              (128, D)))
    return wat, gwt, rw, w1t, w2t, w3t


EYE_NP = np.eye(128, dtype=np.float32)


def numpy_core_reference(cfg, xb, weight, gate_w, rms_w, W1, W2, W3):
    c = derived(cfg)
    T, TQ, DE = c["T"], c["TQ"], c["DE"]
    x = xb.astype(np.float64)

    def ln(a, eps=LN_EPS):
        m = a.mean(-1, keepdims=True)
        v = a.var(-1, keepdims=True)
        return (a - m) / np.sqrt(v + eps)

    h1 = ln(x)
    o = np.zeros((TQ, NH * D))
    for h in range(NH):
        yq = h1[:TQ, h * 64:(h + 1) * 64]
        yk = h1[:, h * 64:(h + 1) * 64]
        s = yq @ yk.T / 8.0
        a = np.exp(s - s.max(-1, keepdims=True))
        a = a / a.sum(-1, keepdims=True)
        o[:, h * D:(h + 1) * D] = a @ h1
    attn = o @ weight.astype(np.float64)
    x2 = x[:TQ] + ln(attn)
    h2 = ln(x2)
    y = h2 / np.sqrt((h2 ** 2).mean(-1, keepdims=True) + RMS_EPS) * rms_w.astype(np.float64)
    logits = y @ gate_w.astype(np.float64).T
    idx = np.argsort(-logits, axis=-1)[:, :2]
    vals = np.take_along_axis(logits, idx, axis=-1)
    w = np.exp(vals - vals.max(-1, keepdims=True))
    w = w / w.sum(-1, keepdims=True)
    wfull = np.zeros_like(logits)
    np.put_along_axis(wfull, idx, w, axis=-1)
    comb = np.zeros((TQ, D))
    for e in range(NE):
        up = y @ W1[e].astype(np.float64).T
        gate = y @ W2[e].astype(np.float64).T
        hh = up / (1 + np.exp(-up)) * gate
        oe = hh @ W3[e].astype(np.float64).T
        comb += oe * wfull[:, e:e + 1]
    ffn = h2 + comb
    return x2 + ln(ffn)


_NC_CACHE = {}


def _get_program():
    key = "full"
    if key not in _NC_CACHE:
        _NC_CACHE[key] = build_program(FULL)
    return _NC_CACHE[key]


def kernel(L0_attn_x, L0_attn_weight, L0_ffn_gate_w, L0_ffn_rms_w,
           L0_ffn_W1, L0_ffn_W2, L0_ffn_W3):
    from concourse.bass_utils import run_bass_kernel_spmd
    cfg = FULL
    c = derived(cfg)
    T, TQ = c["T"], c["TQ"]
    x = np.asarray(L0_attn_x, dtype=np.float32)
    wat, gwt, rw, w1t, w2t, w3t = prep_shared(
        cfg, np.asarray(L0_attn_weight, dtype=np.float32),
        np.asarray(L0_ffn_gate_w, dtype=np.float32),
        np.asarray(L0_ffn_rms_w, dtype=np.float32),
        np.asarray(L0_ffn_W1, dtype=np.float32),
        np.asarray(L0_ffn_W2, dtype=np.float32),
        np.asarray(L0_ffn_W3, dtype=np.float32))
    in_maps = []
    for core in range(8):
        b, q = divmod(core, 4)
        xb = np.ascontiguousarray(np.roll(x[b], -q * TQ, axis=0))
        in_maps.append({"xb": xb, "wat": wat, "w1t": w1t, "w2t": w2t,
                        "w3t": w3t, "gwt": gwt, "rmswb": rw, "eye": EYE_NP})
    nc = _get_program()
    res = run_bass_kernel_spmd(nc, in_maps, core_ids=list(range(8)))
    out = np.empty((2, T, D), dtype=np.float32)
    for core in range(8):
        b, q = divmod(core, 4)
        out[b, q * TQ:(q + 1) * TQ] = np.asarray(res.results[core]["out"])
    return out
